# version 26
# baseline (speedup 1.0000x reference)
import numpy as np
import ml_dtypes
import concourse.bass as bass
import concourse.mybir as mybir
from concourse.bass_utils import run_bass_kernel_spmd
from contextlib import ExitStack

F32 = mybir.dt.float32
BF16 = mybir.dt.bfloat16
AF = mybir.ActivationFunctionType
ALU = mybir.AluOpType

L = 4096
D = 2048
NDC = D // 128
H = 16
DH = 128
RK = 256
NRC = RK // 128
FF = 4096
NFC = FF // 128
T = 512
NT = L // T
NG = T // 128
NKC = L // 128
EPS_LN = 1e-5
EPS_RMS = 1e-6
QSCALE = 1.0 / np.sqrt(128.0)

P1_STAGES = ("ln1", "ffn", "ln2", "latent", "q")
COMPUTE_ENGINES = ("tensor", "vector", "scalar", "gpsimd")
ALL_ENGINES = COMPUTE_ENGINES + ("sync",)


class SB:

    def __init__(self, nc):
        self.nc = nc
        self.ops = []
        self.lastw = {}
        self.readers = {}

    def add(self, eng, fn, reads=(), writes=(), deps=(), dma=False, tag=""):
        dep = set(deps)
        for r in reads:
            w = self.lastw.get(r)
            if w is not None:
                dep.add(w)
        i = len(self.ops)
        for r in reads:
            self.readers.setdefault(r, set()).add(i)
        for w in writes:
            lw = self.lastw.get(w)
            if lw is not None:
                dep.add(lw)
            dep.update(x for x in self.readers.get(w, ()) if x != i)
        self.ops.append(dict(eng=eng, fn=fn, deps=dep, dma=dma, tag=tag))
        for w in writes:
            self.lastw[w] = i
            self.readers[w] = set()
        return i

    def dma(self, eng, fn, reads=(), writes=(), deps=(), tag=""):
        return self.add(eng, fn, reads, writes, deps, dma=True, tag=tag)

    def emit(self, block, es):
        nc = self.nc
        n = len(self.ops)
        csem = {e: es.enter_context(nc.semaphore(f"c_{e}")) for e in COMPUTE_ENGINES}
        qsem = {e: es.enter_context(nc.semaphore(f"q_{e}")) for e in ALL_ENGINES}

        filt = [None] * n
        needed = [False] * n
        for i, op in enumerate(self.ops):
            keep = []
            for d in op["deps"]:
                p = self.ops[d]
                if p["dma"]:
                    if op["dma"] and p["eng"] == op["eng"]:
                        continue
                    keep.append(d)
                else:
                    if p["eng"] == op["eng"] and not op["dma"]:
                        if p["eng"] == "tensor":
                            continue
                        keep.append(d)
                    else:
                        keep.append(d)
            filt[i] = keep
            for d in keep:
                needed[d] = True
        for i, op in enumerate(self.ops):
            if op["dma"]:
                needed[i] = True
        val = [0] * n
        counts = {}
        for i, op in enumerate(self.ops):
            if needed[i]:
                if op["dma"]:
                    key = ("q", op["eng"])
                    counts[key] = counts.get(key, 0) + 16
                else:
                    key = ("c", op["eng"])
                    counts[key] = counts.get(key, 0) + 1
                val[i] = counts[key]
        totals = dict(counts)

        semof = lambda i: (qsem[self.ops[i]["eng"]] if self.ops[i]["dma"]
                           else csem[self.ops[i]["eng"]])
        semkey = lambda i: (("q", self.ops[i]["eng"]) if self.ops[i]["dma"]
                            else ("c", self.ops[i]["eng"]))

        def run_engine(e, eng_name):
            waited = {}
            for i, op in enumerate(self.ops):
                if op["eng"] != eng_name:
                    continue
                tgt = {}
                for d in filt[i]:
                    k = semkey(d)
                    tgt[k] = max(tgt.get(k, 0), val[d])
                if op["dma"]:
                    k = ("q", eng_name)
                    v = val[i] - 16
                    if v > 0:
                        tgt[k] = max(tgt.get(k, 0), v)
                for k, v in sorted(tgt.items()):
                    if waited.get(k, 0) >= v:
                        continue
                    e.wait_ge(qsem[k[1]] if k[0] == "q" else csem[k[1]], v)
                    waited[k] = v
                inst = op["fn"](e)
                if needed[i]:
                    inst.then_inc(semof(i), 16 if op["dma"] else 1)
            if eng_name == "sync":
                for k, v in sorted(totals.items()):
                    if waited.get(k, 0) >= v:
                        continue
                    e.wait_ge(qsem[k[1]] if k[0] == "q" else csem[k[1]], v)

        for name in ALL_ENGINES:
            dec = getattr(block, {"tensor": "tensor", "vector": "vector",
                                  "scalar": "scalar", "gpsimd": "gpsimd",
                                  "sync": "sync"}[name])
            dec(lambda e, _n=name: run_engine(e, _n))


def build_nc(debug=False, phases=(0, 1, 2, 3), nt1=NT, nheads=H, nt3=NT, detect_races=True):
    nc = bass.Bass("TRN2", target_bir_lowering=False, detect_race_conditions=detect_races)

    DI = lambda name, shape, dt=F32: nc.dram_tensor(name, shape, dt, kind="ExternalInput").ap()
    x_in = DI("x", [L, D])
    wq_s = DI("wq_s", [H * 128, NDC * 128])
    wkvd_s = DI("wkvd_s", [128, NDC * NRC * 128])
    wkvu_s = DI("wkvu_s", [128, NRC * 4096])
    wo_s = DI("wo_s", [NDC * 128, NDC * 128])
    wg_s = DI("wg_s", [128, NDC * 2])
    w0a_s = DI("w0a_s", [NFC * 128, NDC * 128])
    w1a_s = DI("w1a_s", [NFC * 128, NDC * 128])
    w0b_s = DI("w0b_s", [NDC * 128, NFC * 128])
    w1b_s = DI("w1b_s", [NDC * 128, NFC * 128])
    b0a_s = DI("b0a_s", [128, NFC])
    b1a_s = DI("b1a_s", [128, NFC])
    b0b_s = DI("b0b_s", [128, NDC])
    b1b_s = DI("b1b_s", [128, NDC])
    kvw_s = DI("kvw_s", [128, NRC])
    id_s = DI("id_s", [128, 128])
    mask_s = DI("mask_s", [128, 4 * 512], BF16)

    y_out = nc.dram_tensor("y", [L, D], F32, kind="ExternalOutput").ap()

    SCR = "ExternalOutput" if debug else "Internal"
    wq_b = nc.dram_tensor("wq_b", [H * 128, NDC * 128], BF16, kind="Internal").ap()
    wo_b = nc.dram_tensor("wo_b", [NDC * 128, NDC * 128], BF16, kind="Internal").ap()
    w0a_b = nc.dram_tensor("w0a_b", [NFC * 128, NDC * 128], BF16, kind="Internal").ap()
    w1a_b = nc.dram_tensor("w1a_b", [NFC * 128, NDC * 128], BF16, kind="Internal").ap()
    w0b_b = nc.dram_tensor("w0b_b", [NDC * 128, NFC * 128], BF16, kind="Internal").ap()
    w1b_b = nc.dram_tensor("w1b_b", [NDC * 128, NFC * 128], BF16, kind="Internal").ap()
    q_scr = nc.dram_tensor("q_scr", [H * 128, L], BF16, kind=SCR).ap()
    r_scr = nc.dram_tensor("r_scr", [NDC * 128, L], BF16, kind=SCR).ap()
    o_scr = nc.dram_tensor("o_scr", [H * 128, L], BF16, kind=SCR).ap()

    es = ExitStack()
    with es:
        S = lambda name, shape, dt: es.enter_context(nc.sbuf_tensor(name, shape, dt))
        latent = S("latent", [128, NRC * L], BF16)
        wkvd = S("wkvd", [128, NDC * NRC * 128], BF16)
        wg = S("wg", [128, NDC * 2], BF16)
        b0a = S("b0a", [128, NFC], F32)
        b1a = S("b1a", [128, NFC], F32)
        b0b = S("b0b", [128, NDC], F32)
        b1b = S("b1b", [128, NDC], F32)
        kvw = S("kvw", [128, NRC], F32)
        id128 = S("id128", [128, 128], F32)
        masks = S("masks", [128, 4 * 512], BF16)
        ones_col = S("ones_col", [128, 1], BF16)
        onesk1 = S("onesk1", [1, 128], BF16)
        zeros_f = S("zeros_f", [128, 1], F32)
        epsln_f = S("epsln_f", [128, 1], F32)
        epsrms_f = S("epsrms_f", [128, 1], F32)
        A1 = S("A1", [128, 16384], BF16)
        A2 = S("A2", [128, 8192], BF16)
        A3 = S("A3", [128, 8192], BF16)
        A4 = S("A4", [128, 8192], BF16)
        A5 = S("A5", [128, 8192], BF16)
        sqb = S("sqb", [128, 1024], BF16)
        wsa = S("wsa", [128, 2 * 2048], BF16)
        wsb = S("wsb", [128, 2 * 4096], BF16)
        FA = S("FA", [128, 6144], F32)
        F2 = S("F2", [128, 3072], F32)
        st_am = S("st_am", [1, 512], F32)
        st_aq = S("st_aq", [1, 512], F32)
        st_rstd = S("st_rstd", [1, 512], F32)
        st_m2 = st_rstd
        st_d = S("st_d", [1, 512], F32)
        st_ms = st_d
        st_g = S("st_g", [1, 512], F32)
        s16 = S("s16", [1, 16], F32)
        dwg = S("dwg", [128, NDC], BF16)
        sqtab = S("sqtab", [128, H], F32)
        skvdtab = S("skvdtab", [128, NRC], F32)
        bf_am = S("bf_am", [1, 512], BF16)
        bf_rstd = S("bf_rstd", [1, 512], BF16)
        bf_mask = S("bf_mask", [1, 512], BF16)
        bf_rs = bf_am
        st_rs = st_am

        pp0 = es.enter_context(nc.psum_tensor("pp0", [128, 1024], F32))
        pp1 = es.enter_context(nc.psum_tensor("pp1", [128, 1024], F32))
        pb4 = es.enter_context(nc.psum_tensor("pb4", [128, 512], F32))
        pb5 = es.enter_context(nc.psum_tensor("pb5", [128, 512], F32))
        pb6 = es.enter_context(nc.psum_tensor("pb6", [128, 512], F32))
        pb7 = es.enter_context(nc.psum_tensor("pb7", [128, 512], F32))
        PPH = [pp0[:, 0:512], pp0[:, 512:1024]]
        PPO = [pp1[:, 0:512], pp1[:, 512:1024]]
        KH = ["pp0L", "pp0R"]
        KO = ["pp1L", "pp1R"]

        sb = SB(nc)

        def cp(dst, src, reads, writes, eng="vector", tag="cp"):
            sb.add(eng, lambda e, d=dst, s=src: e.tensor_copy(d, s), reads, writes, tag=tag)

        def convert_direct(src_ap, dst_sb, key, ncols, extra_writes=()):
            for c in range((ncols + 1023) // 1024):
                w = min(1024, ncols - c * 1024)
                par = c % 2
                stg = FA[:, 4096 + par * 1024: 4096 + par * 1024 + w]
                sb.dma("sync", lambda e, s=stg, a=src_ap[:, c * 1024: c * 1024 + w]: e.dma_start(out=s, in_=a),
                       reads=[], writes=[("cvtf", par)], tag="cvt_ld")
                sb.add("vector", lambda e, d=dst_sb[:, c * 1024: c * 1024 + w], s=stg: e.tensor_copy(d, s),
                       reads=[("cvtf", par)], writes=[key] + list(extra_writes), tag="cvt_cp")

        convq = []
        conv_done = set()
        conv_n = [0]
        pair_reg = {}
        latest_par = {}

        def conv_pair(src_ap, dst_ap, key, rr, c0):
            par = conv_n[0] % 2
            conv_n[0] += 1
            pairbuf = A5[:, 0:2048] if par == 0 else A5[:, 5120:7168]
            seq = conv_n[0] - 1
            cvtf_bufs = ((FA[:, 4096:5120], [("cvtf", 0)]),
                         (FA[:, 5120:6144], [("cvtf", 1)]),
                         (F2[:, 1536:2560], [("cvtf", 2), ("cs", 0), ("cs", 1)]))
            for half, ldq in ((0, "sync"), (1, "gpsimd")):
                slot = (2 * seq + half) % 3
                stf, skeys = cvtf_bufs[slot]
                ceng = ("gpsimd", "scalar", "vector")[slot]
                src = src_ap[rr * 128:(rr + 1) * 128, (c0 + half) * 1024:(c0 + half + 1) * 1024]
                sb.dma(ldq, lambda e, s=stf, a=src: e.dma_start(out=s, in_=a),
                       reads=[], writes=list(skeys), tag="cvt_ld")
                if ceng == "scalar":
                    sb.add("scalar", lambda e, d=pairbuf[:, half * 1024:(half + 1) * 1024], s=stf:
                           e.activation(d, s, AF.Copy, bias=0.0, scale=1.0),
                           reads=list(skeys), writes=[("cvtb", par, half)], tag="cvt_cp")
                else:
                    sb.add(ceng, lambda e, d=pairbuf[:, half * 1024:(half + 1) * 1024], s=stf:
                           e.tensor_copy(d, s),
                           reads=list(skeys), writes=[("cvtb", par, half)], tag="cvt_cp")
            sb.dma("scalar", lambda e, d=dst_ap[rr * 128:(rr + 1) * 128, c0 * 1024:(c0 + 2) * 1024], s=pairbuf:
                   e.dma_start(out=d, in_=s),
                   reads=[("cvtb", par, 0), ("cvtb", par, 1)],
                   writes=[(key, rr, c0), (key, rr, c0 + 1)], tag="cvt_st")
            conv_done.add((key, rr, c0))
            conv_done.add((key, rr, c0 + 1))
            pair_reg[(key, rr, c0)] = (par, conv_n[0])
            latest_par[par] = conv_n[0]

        def convert_scratch(src_ap, dst_ap, key, nrows, ncols):
            assert ncols % 2048 == 0
            for rr in range(nrows // 128):
                for c0 in range(0, ncols // 1024, 2):
                    convq.append(lambda rr=rr, c0=c0: conv_pair(src_ap, dst_ap, key, rr, c0))

        def pump(k):
            for _ in range(k):
                if convq:
                    convq.pop(0)()

        def need(*keys):
            while convq and not all(k in conv_done for k in keys):
                pump(1)

        def fresh_pair(key, rr, c0):
            ent = pair_reg.get((key, rr, c0))
            if ent is not None and latest_par.get(ent[0]) == ent[1]:
                return A5[:, 0:2048] if ent[0] == 0 else A5[:, 5120:7168], ent[0]
            return None

        def load_or_copy(dst, src_dram, key, rr, ncp, wkeys, direct, tag, queue="sync"):
            if direct:
                fps = [fresh_pair(key, rr, 2 * k) for k in range(ncp)]
                if all(f is not None for f in fps):
                    for k, (pb_, par_) in enumerate(fps):
                        sb.add("vector", lambda e, d=dst[:, k * 2048:(k + 1) * 2048], s=pb_:
                               e.tensor_copy(d, s),
                               reads=[("cvtb", par_, 0), ("cvtb", par_, 1)],
                               writes=list(wkeys), tag=tag + "c")
                    return
            sb.dma(queue, lambda e, d=dst, a=src_dram: e.dma_start(out=d, in_=a),
                   reads=[(key, rr, c) for c in range(2 * ncp)], writes=list(wkeys), tag=tag)

        def prelude():
            sb.add("vector", lambda e: e.memset(ones_col[:], 1.0), writes=[("ones",)])
            sb.add("vector", lambda e: e.memset(onesk1[:], 1.0), writes=[("onesk1",)])
            sb.add("vector", lambda e: e.memset(zeros_f[:], 0.0), writes=[("zeros",)])
            sb.add("vector", lambda e: e.memset(epsln_f[:], EPS_LN), writes=[("epsln",)])
            sb.add("vector", lambda e: e.memset(epsrms_f[:], EPS_RMS), writes=[("epsrms",)])
            for (ap_, sbuf, key) in ((id_s, id128, ("id",)), (mask_s, masks, ("masks",)),
                                     (b0a_s, b0a, ("b0a",)), (b1a_s, b1a, ("b1a",)),
                                     (b0b_s, b0b, ("b0b",)), (b1b_s, b1b, ("b1b",)),
                                     (kvw_s, kvw, ("kvw",))):
                sb.dma("gpsimd", lambda e, d=sbuf, s=ap_: e.dma_start(out=d[:], in_=s), writes=[key], tag="cst_ld")
            convert_direct(wg_s, wg, ("wg",), NDC * 2)
            convert_direct(wkvd_s, wkvd, ("wkvd",), NDC * NRC * 128)
            for c in range(NDC):
                sb.add("vector", lambda e, c=c: e.tensor_tensor(dwg[:, c:c + 1], wg[:, 2 * c:2 * c + 1],
                                                                wg[:, 2 * c + 1:2 * c + 2], ALU.subtract),
                       reads=[("wg",)], writes=[("dwg",)], tag="dwg")
            sb.add("tensor", lambda e: e.matmul(pb6[0:1, 0:NDC], ones_col[:], dwg[:], start=True, stop=True),
                   reads=[("dwg",), ("ones",)], writes=["pb6"], tag="dS_mm")
            sb.add("vector", lambda e: e.tensor_copy(s16[:], pb6[0:1, 0:NDC]),
                   reads=["pb6"], writes=["s16"], tag="dS_ev")
            for hw in (8, 4, 2, 1):
                sb.add("vector", lambda e, hw=hw: e.tensor_tensor(s16[:, 0:hw], s16[:, 0:hw], s16[:, hw:2 * hw], ALU.add),
                       reads=["s16"], writes=["s16"], tag="dS_add")
            for k in range(8):
                sb.add("tensor", lambda e, k=k: e.matmul(pb4[0:1, :], ones_col[:], wkvd[:, k * 512:(k + 1) * 512],
                                                         start=(k == 0), stop=(k == 7)),
                       reads=[("wkvd",), ("ones",)], writes=["pb4"], tag="skvd_mm")
            sb.add("vector", lambda e: e.tensor_copy(st_g[:], pb4[0:1, :]), reads=["pb4"], writes=["st_g"], tag="skvd_ev")
            sb.add("vector", lambda e: e.tensor_tensor(st_g[:, 0:128], st_g[:, 0:128], st_g[:, 256:384], ALU.add),
                   reads=["st_g"], writes=["st_g"], tag="skvd_f0")
            sb.add("vector", lambda e: e.tensor_tensor(st_g[:, 128:256], st_g[:, 128:256], st_g[:, 384:512], ALU.add),
                   reads=["st_g"], writes=["st_g"], tag="skvd_f1")
            sb.add("vector", lambda e: e.tensor_copy(bf_am[:, 0:256], st_g[:, 0:256]),
                   reads=["st_g"], writes=["bf_am"], tag="skvd_bf")
            for rc in range(NRC):
                sb.add("tensor", lambda e, rc=rc: e.matmul(pb5[:, rc:rc + 1], bf_am[:, rc * 128:(rc + 1) * 128],
                                                           onesk1[:, 0:1], start=True, stop=True),
                       reads=["bf_am", ("onesk1",)], writes=["pb5"], tag="skvd_tr")
            sb.add("vector", lambda e: e.tensor_copy(skvdtab[:], pb5[:, 0:NRC]),
                   reads=["pb5"], writes=["skvdtab"], tag="skvd_te")
            convert_scratch(w0a_s, w0a_b, "w0a_b", NFC * 128, NDC * 128)
            convert_scratch(w0b_s, w0b_b, "w0b_b", NDC * 128, NFC * 128)
            convert_scratch(w1a_s, w1a_b, "w1a_b", NFC * 128, NDC * 128)
            convert_scratch(w1b_s, w1b_b, "w1b_b", NDC * 128, NFC * 128)
            convert_scratch(wq_s, wq_b, "wq_b", H * 128, NDC * 128)
            convert_scratch(wo_s, wo_b, "wo_b", NDC * 128, NDC * 128)
            pump(8)

        def ln_stats_chunk(c, rkeys, rview, tag, with_gate=False, sq_eng="scalar"):
            par = c % 2
            sq = sqb[:, par * 512:(par + 1) * 512]
            sb.add("tensor", lambda e, c=c: e.matmul(pb4[0:1, :], ones_col[:], rview(c),
                                                     start=(c == 0), stop=(c == NDC - 1)),
                   reads=[rkeys(c), ("ones",)], writes=["pb4"], tag=f"{tag}_mean")
            if sq_eng == "scalar":
                sb.add("scalar", lambda e, s=sq, c=c: e.activation(s, rview(c), AF.Square, bias=0.0, scale=1.0),
                       reads=[rkeys(c)], writes=[("sq", par)], tag=f"{tag}_sq")
            else:
                sb.add("gpsimd", lambda e, s=sq, c=c: e.tensor_tensor(s, rview(c), rview(c), ALU.mult),
                       reads=[rkeys(c)], writes=[("sq", par)], tag=f"{tag}_sq")
            sb.add("tensor", lambda e, s=sq, c=c: e.matmul(pb5[0:1, :], ones_col[:], s,
                                                           start=(c == 0), stop=(c == NDC - 1)),
                   reads=[("sq", par), ("ones",)], writes=["pb5"], tag=f"{tag}_sumsq")
            if with_gate:
                sb.add("tensor", lambda e, c=c: e.matmul(PPO[0][0:1, :], dwg[:, c:c + 1], rview(c),
                                                         start=(c == 0), stop=(c == NDC - 1)),
                       reads=[rkeys(c), ("dwg",)], writes=[KO[0]], tag=f"{tag}_gmm")

        def ln_finish(rkeys, rview, xnview, xnkeys, tag, make_xn=True):
            sb.add("vector", lambda e: e.tensor_scalar(st_am[:], pb4[0:1, :], 1.0 / D, None, ALU.mult),
                   reads=["pb4"], writes=["st_am"], tag=f"{tag}_am")
            sb.add("vector", lambda e: e.tensor_scalar(st_aq[:], pb5[0:1, :], 1.0 / D, None, ALU.mult),
                   reads=["pb5"], writes=["st_aq"], tag=f"{tag}_aq")
            sb.add("vector", lambda e: e.tensor_tensor(st_m2[:], st_am[:], st_am[:], ALU.mult),
                   reads=["st_am"], writes=["st_rstd"], tag=f"{tag}_m2")
            sb.add("vector", lambda e: e.tensor_tensor(st_aq[:], st_aq[:], st_m2[:], ALU.subtract),
                   reads=["st_aq", "st_rstd"], writes=["st_aq"], tag=f"{tag}_var")
            sb.add("scalar", lambda e: e.activation(st_rstd[:], st_aq[:], AF.Ln, bias=epsln_f[0:1, :], scale=1.0),
                   reads=["st_aq", ("epsln",)], writes=["st_rstd"], tag=f"{tag}_ln")
            sb.add("scalar", lambda e: e.activation(st_rstd[:], st_rstd[:], AF.Exp, bias=zeros_f[0:1, :], scale=-0.5),
                   reads=["st_rstd", ("zeros",)], writes=["st_rstd"], tag=f"{tag}_exp")
            sb.add("vector", lambda e: e.tensor_copy(bf_am[:], st_am[:]),
                   reads=["st_am"], writes=["bf_am"], tag=f"{tag}_ambf")
            sb.add("vector", lambda e: e.tensor_copy(bf_rstd[:], st_rstd[:]),
                   reads=["st_rstd"], writes=["bf_rstd"], tag=f"{tag}_rsbf")
            sb.add("tensor", lambda e: e.matmul(pb4[:], onesk1[:], bf_am[:], start=True, stop=True),
                   reads=["bf_am", ("onesk1",)], writes=["pb4"], tag=f"{tag}_bcm")
            sb.add("tensor", lambda e: e.matmul(pb5[:], onesk1[:], bf_rstd[:], start=True, stop=True),
                   reads=["bf_rstd", ("onesk1",)], writes=["pb5"], tag=f"{tag}_bcr")
            mb = F2[:, 0:512]
            rb = F2[:, 512:1024]
            sb.add("vector", lambda e: e.tensor_copy(mb, pb4[:]), reads=["pb4"], writes=["mb"], tag=f"{tag}_mbe")
            sb.add("vector", lambda e: e.tensor_copy(rb, pb5[:]), reads=["pb5"], writes=["rb"], tag=f"{tag}_rbe")
            if not make_xn:
                return
            for c in range(NDC):
                par = c % 2
                tmp = sqb[:, par * 512:(par + 1) * 512]
                sb.add("gpsimd", lambda e, t=tmp, c=c: e.tensor_tensor(t, rview(c), mb, ALU.subtract),
                       reads=[rkeys(c), "mb"], writes=[("sq", par)], tag=f"{tag}_sub")
                sb.add("vector", lambda e, t=tmp, c=c: e.tensor_tensor(xnview(c), t, rb, ALU.mult),
                       reads=[("sq", par), "rb"], writes=[xnkeys(c)], tag=f"{tag}_mul")

        def gate_combine(tag):
            sb.add("vector", lambda e: e.tensor_copy(st_g[:], PPO[0][0:1, :]),
                   reads=[KO[0]], writes=["st_g"], tag=f"{tag}_gev")
            sb.add("vector", lambda e: e.tensor_scalar(st_d[:], st_am[:], s16[:, 0:1], None, ALU.mult),
                   reads=["st_am", "s16"], writes=["st_d"], tag=f"{tag}_gms")
            sb.add("vector", lambda e: e.tensor_tensor(st_d[:], st_g[:], st_d[:], ALU.subtract),
                   reads=["st_g", "st_d"], writes=["st_d"], tag=f"{tag}_gd")
            sb.add("vector", lambda e: e.tensor_scalar(st_d[:], st_d[:], 0.0, None, ALU.is_ge),
                   reads=["st_d"], writes=["st_d"], tag=f"{tag}_gm")
            sb.add("vector", lambda e: e.tensor_copy(bf_mask[:], st_d[:]),
                   reads=["st_d"], writes=["bf_mask"], tag=f"{tag}_gmb")
            sb.add("tensor", lambda e: e.matmul(pb6[:], onesk1[:], bf_mask[:], start=True, stop=True),
                   reads=["bf_mask", ("onesk1",)], writes=["pb6"], tag=f"{tag}_gbc")
            maskb = F2[:, 1024:1536]
            sb.add("vector", lambda e: e.tensor_copy(maskb, pb6[:]), reads=["pb6"], writes=["maskb"], tag=f"{tag}_gbe")

        def ffn(xnview, xnkeys, tag, out_cb, direct=False):
            GB = [(PPH[0], KH[0]), (PPH[1], KH[1]), (PPO[0], KO[0]), (PPO[1], KO[1]),
                  (pb4[:], "pb4"), (pb5[:], "pb5")]
            for eidx in (0, 1):
                wa_b = w0a_b if eidx == 0 else w1a_b
                wb_b = w0b_b if eidx == 0 else w1b_b
                ba = b0a if eidx == 0 else b1a
                bb = b0b if eidx == 0 else b1b
                ba_key = ("b0a",) if eidx == 0 else ("b1a",)
                bb_key = ("b0b",) if eidx == 0 else ("b1b",)
                wa_key = "w0a_b" if eidx == 0 else "w1a_b"
                wb_key = "w0b_b" if eidx == 0 else "w1b_b"
                NG0 = 6 if eidx == 0 else 0
                if NG0:
                    gw = [wsa[:, 0:2048], wsa[:, 2048:4096],
                          A4[:, 0:2048], A4[:, 2048:4096], A4[:, 4096:6144], A4[:, 6144:8192]]
                    for j in range(NG0):
                        extra = [("o0", 4 * (j - 2)), ("o0", 4 * (j - 2) + 1), ("o0", 4 * (j - 2) + 2),
                                 ("o0", 4 * (j - 2) + 3)] if j >= 2 else []
                        wkey = ("wsa", j) if j < 2 else ("wsa2", j)
                        need((wa_key, j, 0), (wa_key, j, 1))
                        load_or_copy(gw[j], wa_b[j * 128:(j + 1) * 128, :], wa_key, j, 1,
                                     [wkey] + extra, direct, f"{tag}_wal")
                        pump(2)
                    for c in range(NDC):
                        for j in range(NG0):
                            pb_, pk = GB[j]
                            wkey = ("wsa", j) if j < 2 else ("wsa2", j)
                            sb.add("tensor", lambda e, c=c, j=j, pb_=pb_:
                                   e.matmul(pb_, gw[j][:, c * 128:(c + 1) * 128],
                                            xnview(c), start=(c == 0), stop=(c == NDC - 1)),
                                   reads=[wkey, xnkeys(c)], writes=[pk], tag=f"{tag}_h{eidx}")
                    for j in range(NG0):
                        pb_, pk = GB[j]
                        hv = A1[:, j * 512:(j + 1) * 512]
                        sb.add("scalar", lambda e, h=hv, j=j, pb_=pb_: e.activation(h, pb_, AF.Gelu,
                                                                                   bias=ba[:, j:j + 1], scale=1.0),
                               reads=[pk, ba_key], writes=[("h", j)], tag=f"{tag}_ge")
                for j in range(NG0, NFC):
                    par = j % 2
                    wbuf = wsa[:, par * 2048:(par + 1) * 2048]
                    need((wa_key, j, 0), (wa_key, j, 1))
                    load_or_copy(wbuf, wa_b[j * 128:(j + 1) * 128, :], wa_key, j, 1,
                                 [("wsa", par)], direct, f"{tag}_wal")
                    pump(2)
                    for c in range(NDC):
                        sb.add("tensor", lambda e, c=c, par=par, j=j:
                               e.matmul(PPH[j % 2], wsa[:, (j % 2) * 2048 + c * 128:(j % 2) * 2048 + (c + 1) * 128],
                                        xnview(c), start=(c == 0), stop=(c == NDC - 1)),
                               reads=[("wsa", par), xnkeys(c)], writes=[KH[j % 2]], tag=f"{tag}_h{eidx}")
                    hv = A1[:, j * 512:(j + 1) * 512]
                    if eidx == 0:
                        sb.add("scalar", lambda e, h=hv, j=j: e.activation(h, PPH[j % 2], AF.Gelu,
                                                                           bias=ba[:, j:j + 1], scale=1.0),
                               reads=[KH[j % 2], ba_key], writes=[("h", j)], tag=f"{tag}_ge")
                    else:
                        sb.add("vector", lambda e, h=hv, j=j: e.tensor_scalar(h, PPH[j % 2], ba[:, j:j + 1], 0.0,
                                                                              ALU.add, ALU.max),
                               reads=[KH[j % 2], ba_key], writes=[("h", j)], tag=f"{tag}_re")
                for i in range(NDC):
                    par = i % 2
                    wbuf = wsb[:, par * 4096:(par + 1) * 4096]
                    need((wb_key, i, 0), (wb_key, i, 1), (wb_key, i, 2), (wb_key, i, 3))
                    load_or_copy(wbuf, wb_b[i * 128:(i + 1) * 128, :], wb_key, i, 2,
                                 [("wsb", par)], direct, f"{tag}_wbl", queue="scalar")
                    pump(3)
                    for j in range(NFC):
                        sb.add("tensor", lambda e, i=i, j=j, par=par:
                               e.matmul(PPO[i % 2], wsb[:, par * 4096 + j * 128: par * 4096 + (j + 1) * 128],
                                        A1[:, j * 512:(j + 1) * 512], start=(j == 0), stop=(j == NFC - 1)),
                               reads=[("wsb", par), ("h", j)], writes=[KO[i % 2]], tag=f"{tag}_o{eidx}")
                    if eidx == 0:
                        ov = A4[:, i * 512:(i + 1) * 512]
                        sb.add("vector", lambda e, o=ov, i=i: e.tensor_scalar(o, PPO[i % 2], bb[:, i:i + 1], 0.5,
                                                                              ALU.add, ALU.mult),
                               reads=[KO[i % 2], bb_key], writes=[("o0", i)], tag=f"{tag}_oe0")
                    else:
                        out_cb(i, i % 2, bb)

        rview = lambda c: A3[:, c * 512:(c + 1) * 512]
        rkey = lambda c: ("r", c)
        xnview = lambda c: A2[:, c * 512:(c + 1) * 512]
        xnkey = lambda c: ("xn", c)

        def p1_xload(t, g):
            par = g % 2
            xin = FA[:, par * 2048:(par + 1) * 2048]
            sb.dma(("gpsimd", "sync")[par], lambda e, d=xin, a=x_in[t * T + g * 128: t * T + (g + 1) * 128, :]:
                   e.dma_start(out=d, in_=a), writes=[("xin", par)], tag="p1_xld")

        def p1_tile(t):
            stages = P1_STAGES
            for g in range(NG):
                par = g % 2
                xin = FA[:, par * 2048:(par + 1) * 2048]
                if g >= 2:
                    p1_xload(t, g)
                pump(1)
                for c in range(NDC):
                    pbt = pb6 if c % 2 == 0 else pb7
                    pkey = "pb6" if c % 2 == 0 else "pb7"
                    sb.add("tensor", lambda e, s=xin[:, c * 128:(c + 1) * 128], p=pbt:
                           e.transpose(p[:, 0:128], s, id128[:]),
                           reads=[("xin", par), ("id",)], writes=[pkey], tag="p1_xtr")
                    sb.add("vector", lambda e, p=pbt, d=rview(c)[:, g * 128:(g + 1) * 128]:
                           e.tensor_copy(d, p[:, 0:128]),
                           reads=[pkey], writes=[rkey(c)], tag="p1_xev")
                    if g == NG - 1 and "ln1" in stages and c > 1:
                        ln_stats_chunk(c - 2, rkey, rview, f"p1ln_{t}", with_gate=True)
            if "ln1" not in stages:
                return
            ln_stats_chunk(NDC - 2, rkey, rview, f"p1ln_{t}", with_gate=True)
            ln_stats_chunk(NDC - 1, rkey, rview, f"p1ln_{t}", with_gate=True)
            ln_finish(rkey, rview, xnview, xnkey, f"p1ln_{t}")
            gate_combine(f"p1g_{t}")

            def p1_out(i, par, bb):
                o1t = A5[:, 2048 + par * 512: 2048 + (par + 1) * 512]
                dt_ = A5[:, 3072:3584] if par == 0 else A5[:, 3584:4096]
                sb.add("vector", lambda e, o=o1t, i=i, par=par:
                       e.tensor_scalar(o, PPO[par], bb[:, i:i + 1], 0.5, ALU.add, ALU.mult),
                       reads=[KO[par], ("b1b",)], writes=[("o1t", par)], tag="p1_oe1")
                sb.add("vector", lambda e, d=dt_, o=o1t, i=i:
                       e.tensor_tensor(d, A4[:, i * 512:(i + 1) * 512], o, ALU.subtract),
                       reads=[("o1t", par), ("o0", i)], writes=[("dt", par)], tag="p1_df")
                sb.add("vector", lambda e, d=dt_: e.tensor_tensor(d, d, F2[:, 1024:1536], ALU.mult),
                       reads=[("dt", par), "maskb"], writes=[("dt", par)], tag="p1_sel")
                sb.add("vector", lambda e, d=dt_, o=o1t: e.tensor_tensor(d, d, o, ALU.add),
                       reads=[("dt", par), ("o1t", par)], writes=[("dt", par)], tag="p1_cmb")
                sb.add("vector", lambda e, d=dt_, i=i: e.tensor_tensor(rview(i), rview(i), d, ALU.add),
                       reads=[("dt", par), rkey(i)], writes=[rkey(i)], tag="p1_radd")
                if "ln2" in stages and i > 0:
                    ln_stats_chunk(i - 1, rkey, rview, f"p1ln2_{t}", sq_eng="gpsimd")
                if i % 4 == 3:
                    c0 = i - 3
                    sb.dma("gpsimd", lambda e, c0=c0, t=t: e.dma_start(
                        out=r_scr[c0 * 128:(c0 + 4) * 128, t * T:(t + 1) * T].rearrange('(a p) w -> p a w', p=128),
                        in_=A3[:, c0 * 512:(c0 + 4) * 512].rearrange('p (a w) -> p a w', a=4)),
                        reads=[rkey(c0 + k) for k in range(4)],
                        writes=[("r_scr", c0 + k, t) for k in range(4)], tag="p1_rst")

            if "ffn" in stages:
                ffn(xnview, xnkey, f"p1f_{t}", p1_out)
            if t + 1 < NT:
                p1_xload(t + 1, 0)
                p1_xload(t + 1, 1)
            if "ln2" not in stages:
                return
            for hh in (0, 1):
                need(("wq_b", hh, 0), ("wq_b", hh, 1))
                load_or_copy(wsa[:, (hh % 2) * 2048:((hh % 2) + 1) * 2048], wq_b[hh * 128:(hh + 1) * 128, :],
                             "wq_b", hh, 1, [("wsa", hh % 2)], t == 0, "p1_wql", queue="scalar")
            ln_stats_chunk(NDC - 1, rkey, rview, f"p1ln2_{t}", sq_eng="gpsimd")
            ln_finish(rkey, rview, xnview, xnkey, f"p1ln2_{t}", make_xn=False)
            rbq = F2[:, 1024:1536]
            sb.add("vector", lambda e: e.tensor_scalar(rbq, F2[:, 512:1024], QSCALE, None, ALU.mult),
                   reads=["rb"], writes=["maskb"], tag="p1_rbq")
            if "latent" not in stages:
                return
            for rc in range(NRC):
                pbt, pkey = (pb4, "pb4") if rc == 0 else (pb5, "pb5")
                for c in range(NDC):
                    sb.add("tensor", lambda e, rc=rc, c=c, pbt=pbt:
                           e.matmul(pbt[:], wkvd[:, (c * NRC + rc) * 128:(c * NRC + rc + 1) * 128], rview(c),
                                    start=(c == 0), stop=(c == NDC - 1)),
                           reads=[("wkvd",), rkey(c)], writes=[pkey], tag="p1_lat")
                corr = F2[:, 2560:3072]
                sb.add("vector", lambda e, rc=rc, corr=corr:
                       e.tensor_scalar(corr, F2[:, 0:512], skvdtab[:, rc:rc + 1], None, ALU.mult),
                       reads=["mb", "skvdtab"], writes=["rb2"], tag="p1_lcor")
                cs = F2[:, 1536 + rc * 512: 1536 + (rc + 1) * 512]
                sb.add("vector", lambda e, d=cs, p=pbt, corr=corr: e.tensor_tensor(d, p[:], corr, ALU.subtract),
                       reads=[pkey, "rb2"], writes=[("cs", rc)], tag="p1_cse")
                sq = sqb[:, (rc % 2) * 512:((rc % 2) + 1) * 512]
                sb.add("gpsimd", lambda e, s=sq, c_=cs: e.tensor_tensor(s, c_, c_, ALU.mult),
                       reads=[("cs", rc)], writes=[("sq", rc % 2)], tag="p1_csq")
                sb.add("tensor", lambda e, s=sq, rc=rc: e.matmul(pb6[0:1, :], ones_col[:], s,
                                                                 start=(rc == 0), stop=(rc == NRC - 1)),
                       reads=[("sq", rc % 2), ("ones",)], writes=["pb6"], tag="p1_mssum")
            sb.add("vector", lambda e: e.tensor_scalar(st_ms[:], pb6[0:1, :], 1.0 / RK, None, ALU.mult),
                   reads=["pb6"], writes=["st_d"], tag="p1_ms")
            sb.add("scalar", lambda e: e.activation(st_ms[:], st_ms[:], AF.Ln, bias=epsrms_f[0:1, :], scale=1.0),
                   reads=["st_d", ("epsrms",)], writes=["st_d"], tag="p1_msl")
            sb.add("scalar", lambda e: e.activation(st_ms[:], st_ms[:], AF.Exp, bias=zeros_f[0:1, :], scale=-0.5),
                   reads=["st_d", ("zeros",)], writes=["st_d"], tag="p1_mse")
            sb.add("vector", lambda e: e.tensor_copy(bf_rstd[:], st_ms[:]),
                   reads=["st_d"], writes=["bf_rstd"], tag="p1_msb")
            sb.add("tensor", lambda e: e.matmul(pb7[:], onesk1[:], bf_rstd[:], start=True, stop=True),
                   reads=["bf_rstd", ("onesk1",)], writes=["pb7"], tag="p1_msbc")
            rb2 = F2[:, 2560:3072]
            sb.add("vector", lambda e: e.tensor_copy(rb2, pb7[:]), reads=["pb7"], writes=["rb2"], tag="p1_rb2")
            for rc in range(NRC):
                cs = F2[:, 1536 + rc * 512: 1536 + (rc + 1) * 512]
                sb.add("vector", lambda e, c_=cs: e.tensor_tensor(c_, c_, rb2, ALU.mult),
                       reads=[("cs", rc), "rb2"], writes=[("cs", rc)], tag="p1_lrm")
                sb.add("scalar", lambda e, c_=cs, rc=rc, t=t:
                       e.activation(latent[:, rc * L + t * T: rc * L + (t + 1) * T], c_, AF.Copy,
                                    scale=kvw[:, rc:rc + 1]),
                       reads=[("cs", rc), ("kvw",)], writes=[("latent", rc, t)], tag="p1_lw")
            if "q" not in stages:
                return
            for hh in range(H):
                par = hh % 2
                if hh >= 2:
                    need(("wq_b", hh, 0), ("wq_b", hh, 1))
                    load_or_copy(wsa[:, par * 2048:(par + 1) * 2048], wq_b[hh * 128:(hh + 1) * 128, :],
                                 "wq_b", hh, 1, [("wsa", par)], t == 0, "p1_wql", queue="scalar")
                pump(2)
                if t == 0:
                    for k in range(4):
                        sb.add("tensor", lambda e, k=k, par=par:
                               e.matmul(pb4[0:1, :], ones_col[:], wsa[:, par * 2048 + k * 512: par * 2048 + (k + 1) * 512],
                                        start=(k == 0), stop=(k == 3)),
                               reads=[("wsa", par), ("ones",)], writes=["pb4"], tag="sq_mm")
                    sb.add("vector", lambda e: e.tensor_copy(st_g[:], pb4[0:1, :]),
                           reads=["pb4"], writes=["st_g"], tag="sq_ev")
                    sb.add("vector", lambda e: e.tensor_tensor(st_g[:, 0:256], st_g[:, 0:256], st_g[:, 256:512], ALU.add),
                           reads=["st_g"], writes=["st_g"], tag="sq_f0")
                    sb.add("vector", lambda e: e.tensor_tensor(st_g[:, 0:128], st_g[:, 0:128], st_g[:, 128:256], ALU.add),
                           reads=["st_g"], writes=["st_g"], tag="sq_f1")
                    sb.add("vector", lambda e: e.tensor_copy(bf_am[:, 0:128], st_g[:, 0:128]),
                           reads=["st_g"], writes=["bf_am"], tag="sq_bf")
                    sb.add("tensor", lambda e, hh=hh: e.matmul(pb5[:, hh:hh + 1], bf_am[:, 0:128],
                                                               onesk1[:, 0:1], start=True, stop=True),
                           reads=["bf_am", ("onesk1",)], writes=["pb5"], tag="sq_tr")
                    sb.add("vector", lambda e, hh=hh: e.tensor_copy(sqtab[:, hh:hh + 1], pb5[:, hh:hh + 1]),
                           reads=["pb5"], writes=["sqtab"], tag="sq_te")
                for c in range(NDC):
                    sb.add("tensor", lambda e, c=c, par=par:
                           e.matmul(PPH[par], wsa[:, par * 2048 + c * 128: par * 2048 + (c + 1) * 128],
                                    rview(c), start=(c == 0), stop=(c == NDC - 1)),
                           reads=[("wsa", par), rkey(c)], writes=[KH[par]], tag="p1_q")
                qo = A5[:, 4096 + par * 512: 4096 + (par + 1) * 512]
                tmpq = F2[:, 1536 + par * 512: 1536 + (par + 1) * 512]
                sb.add("vector", lambda e, tq=tmpq, hh=hh: e.tensor_scalar(tq, F2[:, 0:512], sqtab[:, hh:hh + 1],
                                                                           None, ALU.mult),
                       reads=["mb", "sqtab"], writes=[("cs", par)], tag="p1_qc1")
                sb.add("vector", lambda e, tq=tmpq, par=par: e.tensor_tensor(tq, PPH[par], tq, ALU.subtract),
                       reads=[KH[par], ("cs", par)], writes=[("cs", par)], tag="p1_qc2")
                sb.add("vector", lambda e, q=qo, tq=tmpq: e.tensor_tensor(q, tq, F2[:, 1024:1536], ALU.mult),
                       reads=[("cs", par), "maskb"], writes=[("qo", par)], tag="p1_qe")
                if hh % 2 == 1:
                    sb.dma("gpsimd", lambda e, hh=hh, t=t: e.dma_start(
                        out=q_scr[(hh - 1) * 128:(hh + 1) * 128, t * T:(t + 1) * T].rearrange('(a p) w -> p a w', p=128),
                        in_=A5[:, 4096:5120].rearrange('p (a w) -> p a w', a=2)),
                        reads=[("qo", 0), ("qo", 1)],
                        writes=[("q_scr", hh - 1, t), ("q_scr", hh, t)], tag="p1_qst")

        def kv_bufs(hh):
            b = hh % 3
            if b < 2:
                return A1[:, b * 8192: b * 8192 + 4096], A1[:, b * 8192 + 4096: b * 8192 + 8192]
            return A5[:, 0:4096], A5[:, 4096:8192]

        def kkey(hh, kb):
            return ("ot", kb) if hh % 3 == 2 else ("k", hh % 3, kb)

        def vkey(hh, kc):
            return ("ot", 8 + kc // 4) if hh % 3 == 2 else ("v", hh % 3, kc)

        def qbuf(hh):
            b = hh % 3
            return A2[:, b * 4096:(b + 1) * 4096] if b < 2 else A4[:, 3072:7168]

        def qkey(hh):
            return ("q", hh % 3)

        def p2_emit_q(hh):
            b = hh % 3
            alias = ([("xn", c) for c in range(b * 8, b * 8 + 8)] if b < 2
                     else [("o0", i) for i in range(6, 14)])
            sb.dma("gpsimd", lambda e, hh=hh: e.dma_start(out=qbuf(hh), in_=q_scr[hh * 128:(hh + 1) * 128, :]),
                   reads=[("q_scr", hh, t_) for t_ in range(NT)],
                   writes=[qkey(hh)] + alias, tag="p2_qld")

        def p2_emit_K(hh, kb, spar):
            kvx, _ = kv_bufs(hh)
            pst = pp0 if spar == 0 else pp1
            pk = KH[0] if spar == 0 else KO[0]
            for rc in range(NRC):
                sb.add("tensor", lambda e, rc=rc, kb=kb, hh=hh, pst=pst:
                       e.matmul(pst[:, 0:512], A3[:, rc * 4096 + hh * 128: rc * 4096 + (hh + 1) * 128],
                                latent[:, rc * L + kb * 512: rc * L + (kb + 1) * 512],
                                start=(rc == 0), stop=(rc == NRC - 1)),
                       reads=[("wkvu",), ("latent", rc, kb)], writes=[pk], tag="p2_k")
            extra = [("h", (hh % 3) * 16 + kb)] if hh % 3 < 2 else \
                ([("cvtb", 0, 0), ("cvtb", 0, 1)] if kb < 4 else [("o1t", 0), ("o1t", 1), ("dt", 0), ("dt", 1)])
            sb.add("vector", lambda e, kb=kb, kvx=kvx, pst=pst: e.tensor_copy(kvx[:, kb * 512:(kb + 1) * 512], pst[:, 0:512]),
                   reads=[pk], writes=[kkey(hh, kb)] + extra, tag="p2_ke")

        def p2_emit_V(hh, kc0, spar):
            _, vvx = kv_bufs(hh)
            pst = pp0 if spar == 0 else pp1
            pk = KH[0] if spar == 0 else KO[0]
            for j in range(4):
                kc = kc0 + j
                for rc in range(NRC):
                    sb.add("tensor", lambda e, rc=rc, kc=kc, hh=hh, pst=pst, j=j:
                           e.matmul(pst[:, j * 128:(j + 1) * 128],
                                    latent[:, rc * L + kc * 128: rc * L + (kc + 1) * 128],
                                    A3[:, rc * 4096 + 2048 + hh * 128: rc * 4096 + 2048 + (hh + 1) * 128],
                                    start=(rc == 0), stop=(rc == NRC - 1)),
                           reads=[("wkvu",), ("latent", rc, kc // 4)], writes=[pk], tag="p2_v")
            extra = [("h", (hh % 3) * 16 + 8 + kc0 // 4)] if hh % 3 < 2 else \
                ([("qo", 0), ("qo", 1)] if kc0 < 8 else [("cvtb", 1, 0), ("cvtb", 1, 1)])
            sb.add("vector", lambda e, kc0=kc0, vvx=vvx, pst=pst:
                   e.tensor_copy(vvx[:, kc0 * 128:(kc0 + 4) * 128], pst[:, 0:512]),
                   reads=[pk], writes=[vkey(hh, kc0 + j) for j in range(4)] + extra, tag="p2_ve")

        def p2_emit_S(task, spar):
            hh, qb, kp = task
            pst = pp0 if spar == 0 else pp1
            pskeys = [KH[0], KH[1]] if spar == 0 else [KO[0], KO[1]]
            kvx, _ = kv_bufs(hh)
            qv = qbuf(hh)
            qx = [("o0", 6 + qb)] if hh % 3 == 2 else []
            for half in range(2):
                kc = kp * 2 + half
                sb.add("tensor", lambda e, kc=kc, qb=qb, pst=pst, half=half, kvx=kvx, qv=qv:
                       e.matmul(pst[:, half * 512:(half + 1) * 512], kvx[:, kc * 128:(kc + 1) * 128],
                                qv[:, qb * 512:(qb + 1) * 512], start=True, stop=True),
                       reads=[kkey(hh, kc // 4), qkey(hh)] + qx, writes=[pskeys[half]], tag="p2_s")

        def p2_emit_exp(task, spar):
            hh, qb, kp = task
            pst = pp0 if spar == 0 else pp1
            pskeys = [KH[0], KH[1]] if spar == 0 else [KO[0], KO[1]]
            ptv = A4[:, spar * 1024:(spar + 1) * 1024]
            sb.add("scalar", lambda e, p=ptv, pst=pst: e.activation(p, pst[:], AF.Exp, bias=zeros_f[:], scale=1.0),
                   reads=pskeys + [("zeros",)],
                   writes=[("pt", spar), ("o0", spar * 2), ("o0", spar * 2 + 1)], tag="p2_exp")
            for half in range(2):
                kc = kp * 2 + half
                if kc >= 4 * qb:
                    jj = kc - 4 * qb
                    sb.add("vector", lambda e, p=ptv, half=half, jj=jj:
                           e.tensor_tensor(p[:, half * 512:(half + 1) * 512],
                                           p[:, half * 512:(half + 1) * 512],
                                           masks[:, jj * 512:(jj + 1) * 512], ALU.mult),
                           reads=[("pt", spar), ("masks",)], writes=[("pt", spar)], tag="p2_msk")

        def p2_emit_consume(task, spar):
            hh, qb, kp = task
            par = hh % 2
            kcmax = 4 * (qb + 1)
            _, vvx = kv_bufs(hh)
            ptv = A4[:, spar * 1024:(spar + 1) * 1024]
            pvb, pvkey = (pb4, "pb4") if par == 0 else (pb7, "pb7")
            acc = F2[:, 1536 + par * 512: 1536 + (par + 1) * 512]
            aeng = "vector" if par == 0 else "gpsimd"
            for half in range(2):
                kc = kp * 2 + half
                sb.add("tensor", lambda e, kc=kc, p=ptv, half=half, km=kcmax, pvb=pvb, vvx=vvx:
                       e.matmul(pvb[:], vvx[:, kc * 128:(kc + 1) * 128],
                                p[:, half * 512:(half + 1) * 512],
                                start=(kc == 0), stop=(kc == km - 1)),
                       reads=[vkey(hh, kc), ("pt", spar)], writes=[pvkey], tag="p2_pv")
                if kc == 0:
                    sb.add(aeng, lambda e, p=ptv, half=half, acc=acc:
                           e.tensor_copy(acc, p[:, half * 512:(half + 1) * 512]),
                           reads=[("pt", spar)], writes=[("cs", par)], tag="p2_acc0")
                else:
                    sb.add(aeng, lambda e, p=ptv, half=half, acc=acc:
                           e.tensor_tensor(acc, acc, p[:, half * 512:(half + 1) * 512], ALU.add),
                           reads=[("pt", spar), ("cs", par)], writes=[("cs", par)], tag="p2_acc")

        def emit_epi_a(hh, qb):
            par = hh % 2
            sumb = pb5 if par == 0 else pb6
            st = st_rs if par == 0 else st_aq
            stk = "st_am" if par == 0 else "st_aq"
            bfr = bf_rs if par == 0 else bf_rstd
            bfk = "bf_am" if par == 0 else "bf_rstd"
            acc = F2[:, 1536 + par * 512: 1536 + (par + 1) * 512]
            sqa = sqb[:, par * 512:(par + 1) * 512]
            sb.add("vector" if par == 0 else "gpsimd",
                   lambda e, acc=acc, sqa=sqa: e.tensor_copy(sqa, acc),
                   reads=[("cs", par)], writes=[("sq", par)], tag="p2_accb")
            sb.add("tensor", lambda e, sqa=sqa, sumb=sumb: e.matmul(sumb[0:1, :], ones_col[:], sqa,
                                                                    start=True, stop=True),
                   reads=[("sq", par), ("ones",)], writes=["pb5" if par == 0 else "pb6"], tag="p2_summ")
            sb.add("scalar", lambda e, st=st, sumb=sumb: e.activation(st[:], sumb[0:1, :], AF.Ln,
                                                                      bias=zeros_f[0:1, :], scale=1.0),
                   reads=["pb5" if par == 0 else "pb6", ("zeros",)], writes=[stk], tag="p2_rc")
            sb.add("scalar", lambda e, st=st: e.activation(st[:], st[:], AF.Exp,
                                                           bias=zeros_f[0:1, :], scale=-1.0),
                   reads=[stk, ("zeros",)], writes=[stk], tag="p2_rc2")
            sb.add("vector", lambda e, st=st, bfr=bfr: e.tensor_copy(bfr[:], st[:]),
                   reads=[stk], writes=[bfk], tag="p2_rcb")

        def emit_epi_b(hh, qb):
            par = hh % 2
            sumb = pb5 if par == 0 else pb6
            sumk = "pb5" if par == 0 else "pb6"
            bfr = bf_rs if par == 0 else bf_rstd
            bfk = "bf_am" if par == 0 else "bf_rstd"
            pvb, pvkey = (pb4, "pb4") if par == 0 else (pb7, "pb7")
            bcs = F2[:, 2560:3072] if par == 0 else F2[:, 0:512]
            bck = "rb2" if par == 0 else "mb"
            sb.add("tensor", lambda e, sumb=sumb, bfr=bfr: e.matmul(sumb[:], onesk1[:], bfr[:], start=True, stop=True),
                   reads=[bfk, ("onesk1",)], writes=[sumk], tag="p2_rbc")
            sb.add("vector", lambda e, bcs=bcs, sumb=sumb: e.tensor_copy(bcs, sumb[:]),
                   reads=[sumk], writes=[bck], tag="p2_rbe")
            oo = A4[:, 2048 + par * 512: 2048 + (par + 1) * 512]
            sb.add("vector", lambda e, o=oo, pvb=pvb, bcs=bcs: e.tensor_tensor(o, pvb[:], bcs, ALU.mult),
                   reads=[pvkey, bck], writes=[("oo", par), ("o0", 4 + par)], tag="p2_on")
            sb.dma("gpsimd", lambda e, o=oo, hh=hh, qb=qb: e.dma_start(
                out=o_scr[hh * 128:(hh + 1) * 128, qb * 512:(qb + 1) * 512], in_=o),
                reads=[("oo", par)], writes=[("o_scr", hh, qb)], tag="p2_ost")

        def p2_all(nh):
            entries = []
            for hp in range(nh // 2):
                h0 = 2 * hp
                base = [(qb, kp) for qb in range(NT) for kp in range(2 * (qb + 1))]
                pos = [0, 0]
                cool = [0, 0]
                rt = []
                cur = 0
                while pos[0] < len(base) or pos[1] < len(base):
                    cand = [s for s in (cur, 1 - cur) if pos[s] < len(base)]
                    pick = None
                    for s in cand:
                        if cool[s] == 0:
                            pick = s
                            break
                    if pick is None:
                        pick = cand[0]
                    qb, kp = base[pos[pick]]
                    rt.append(("t", (h0 + pick, qb, kp)))
                    pos[pick] += 1
                    for s in (0, 1):
                        if cool[s]:
                            cool[s] -= 1
                    if kp == 2 * (qb + 1) - 1:
                        cool[pick] = 3
                    cur = 1 - pick
                if hp == 0:
                    entries.append(("q", 0))
                    entries.append(("q", 1))
                    for u in range(8):
                        for hn in (0, 1):
                            entries.append(("k", hn, u))
                            entries.append(("v", hn, u * 4))
                else:
                    entries.append(("q", h0 + 1))
                    for u in range(8):
                        entries.append(("k", h0 + 1, u))
                        entries.append(("v", h0 + 1, u * 4))
                units = []
                if h0 + 2 < nh:
                    units.append(("q", h0 + 2))
                    for u in range(8):
                        units.append(("k", h0 + 2, u))
                        units.append(("v", h0 + 2, u * 4))
                ui = 0
                for i, tsk in enumerate(rt):
                    entries.append(tsk)
                    if i >= 8 and i % 2 == 0 and ui < len(units):
                        entries.append(units[ui])
                        ui += 1
                entries.extend(units[ui:])

            slots = []
            epi_pend = []

            def consume_slot(j):
                kind = slots[j][0]
                if kind != "t":
                    return
                task = slots[j][1]
                p2_emit_consume(task, j % 2)
                hh, pqb, pkp = task
                if pkp == 2 * (pqb + 1) - 1:
                    emit_epi_a(hh, pqb)
                    epi_pend.append((hh, pqb, len(slots) + 1))

            for ent in entries:
                if ent[0] == "q":
                    p2_emit_q(ent[1])
                    continue
                s = len(slots)
                if s >= 2:
                    consume_slot(s - 2)
                slots.append(ent)
                if ent[0] == "t":
                    p2_emit_S(ent[1], s % 2)
                    p2_emit_exp(ent[1], s % 2)
                elif ent[0] == "k":
                    p2_emit_K(ent[1], ent[2], s % 2)
                else:
                    p2_emit_V(ent[1], ent[2], s % 2)
                while epi_pend and epi_pend[0][2] <= s:
                    e0 = epi_pend.pop(0)
                    emit_epi_b(e0[0], e0[1])
            for j in (len(slots) - 2, len(slots) - 1):
                consume_slot(j)
            while epi_pend:
                e0 = epi_pend.pop(0)
                emit_epi_b(e0[0], e0[1])

        def p3_oload(t):
            for a0 in range(0, H, 4):
                sb.dma(("gpsimd", "sync")[(a0 // 4) % 2], lambda e, a0=a0, t=t: e.dma_start(
                    out=A5[:, a0 * 512:(a0 + 4) * 512].rearrange('p (a w) -> p a w', a=4),
                    in_=o_scr[a0 * 128:(a0 + 4) * 128, t * T:(t + 1) * T].rearrange('(a p) w -> p a w', p=128)),
                    reads=[("o_scr", a0 + k, t) for k in range(4)],
                    writes=[("ot", a0 + k) for k in range(4)], tag="p3_old")

        def p3_tile(t):
            otv = lambda a: A5[:, a * 512:(a + 1) * 512]
            for c0 in range(0, NDC, 4):
                sb.dma(("sync", "gpsimd")[(c0 // 4) % 2], lambda e, c0=c0, t=t: e.dma_start(
                    out=A3[:, c0 * 512:(c0 + 4) * 512].rearrange('p (a w) -> p a w', a=4),
                    in_=r_scr[c0 * 128:(c0 + 4) * 128, t * T:(t + 1) * T].rearrange('(a p) w -> p a w', p=128)),
                    reads=[("r_scr", c0 + k, t) for k in range(4)],
                    writes=[rkey(c0 + k) for k in range(4)], tag="p3_rld")
            for dd in range(NDC):
                par = dd % 2
                wbuf = wsb[:, par * 4096: par * 4096 + 2048]
                need(("wo_b", dd, 0), ("wo_b", dd, 1))
                sb.dma("scalar", lambda e, d=wbuf, dd=dd: e.dma_start(out=d, in_=wo_b[dd * 128:(dd + 1) * 128, :]),
                       reads=[("wo_b", dd, 0), ("wo_b", dd, 1)],
                       writes=[("wsb", par)], tag="p3_wol")
                for a in range(H):
                    sb.add("tensor", lambda e, a=a, par=par:
                           e.matmul(PPH[par], wsb[:, par * 4096 + a * 128: par * 4096 + (a + 1) * 128],
                                    otv(a), start=(a == 0), stop=(a == H - 1)),
                           reads=[("wsb", par), ("ot", a)], writes=[KH[par]], tag="p3_wo")
                sb.add("vector", lambda e, dd=dd, par=par: e.tensor_tensor(rview(dd), rview(dd), PPH[par], ALU.add),
                       reads=[KH[par], rkey(dd)], writes=[rkey(dd)], tag="p3_wadd")
                if dd > 1:
                    ln_stats_chunk(dd - 2, rkey, rview, f"p3ln_{t}", with_gate=True, sq_eng="gpsimd")
            if t + 1 < NT:
                p3_oload(t + 1)
            ln_stats_chunk(NDC - 2, rkey, rview, f"p3ln_{t}", with_gate=True, sq_eng="gpsimd")
            ln_stats_chunk(NDC - 1, rkey, rview, f"p3ln_{t}", with_gate=True, sq_eng="gpsimd")
            ln_finish(rkey, rview, xnview, xnkey, f"p3ln_{t}")
            gate_combine(f"p3g_{t}")

            pending = []

            def emit_tr(i, par):
                fin = FA[:, par * 512:(par + 1) * 512]
                for g in range(NG):
                    pbt, pkey = (pb6, "pb6") if g % 2 == 0 else (pb7, "pb7")
                    sb.add("tensor", lambda e, f=fin, g=g, p=pbt: e.transpose(p[:, 0:128], f[:, g * 128:(g + 1) * 128], id128[:]),
                           reads=[("fin", par), ("id",)], writes=[pkey], tag="p3_tr")
                    sb.add("vector", lambda e, g=g, i=i, p=pbt:
                           e.tensor_copy(FA[:, 2560 + g * 512 + (i % 4) * 128: 2560 + g * 512 + (i % 4 + 1) * 128],
                                         p[:, 0:128]),
                           reads=[pkey], writes=[("ystg", g, i % 4)], tag="p3_ye")
                if i % 4 == 3:
                    dd4 = i // 4
                    for g in range(NG):
                        sb.dma("sync", lambda e, g=g, dd4=dd4, t=t: e.dma_start(
                            out=y_out[t * T + g * 128: t * T + (g + 1) * 128, dd4 * 512:(dd4 + 1) * 512],
                            in_=FA[:, 2560 + g * 512: 2560 + (g + 1) * 512]),
                            reads=[("ystg", g, 0), ("ystg", g, 1), ("ystg", g, 2), ("ystg", g, 3)],
                            writes=[("ydma", g)], tag="p3_yst")

            def p3_out(i, par, bb):
                while pending:
                    emit_tr(*pending.pop(0))
                t1 = FA[:, 1024 + par * 512: 1024 + (par + 1) * 512]
                ds = FA[:, 2048:2560]
                fin = FA[:, par * 512:(par + 1) * 512]
                sb.add("vector", lambda e, o=t1, i=i, par=par:
                       e.tensor_scalar(o, PPO[par], bb[:, i:i + 1], 0.5, ALU.add, ALU.mult),
                       reads=[KO[par], ("b1b",)], writes=[("t1", par)], tag="p3_oe1")
                sb.add("vector", lambda e, o=t1, i=i: e.tensor_tensor(ds, A4[:, i * 512:(i + 1) * 512], o, ALU.subtract),
                       reads=[("t1", par), ("o0", i)], writes=[("ds",)], tag="p3_df")
                sb.add("vector", lambda e: e.tensor_tensor(ds, ds, F2[:, 1024:1536], ALU.mult),
                       reads=[("ds",), "maskb"], writes=[("ds",)], tag="p3_sel")
                sb.add("vector", lambda e, o=t1: e.tensor_tensor(ds, ds, o, ALU.add),
                       reads=[("ds",), ("t1", par)], writes=[("ds",)], tag="p3_cmb")
                sb.add("vector", lambda e, f=fin, i=i: e.tensor_tensor(f, rview(i), ds, ALU.add),
                       reads=[("ds",), rkey(i)], writes=[("fin", par)], tag="p3_fin")
                pending.append((i, par))
                if i == NDC - 1:
                    while pending:
                        emit_tr(*pending.pop(0))

            ffn(xnview, xnkey, f"p3f_{t}", p3_out)

        if 0 in phases:
            prelude()
        if 1 in phases:
            p1_xload(0, 0)
            p1_xload(0, 1)
            for t in range(nt1):
                p1_tile(t)
        if 2 in phases:
            pump(len(convq))
            convert_direct(wkvu_s, A3, ("wkvu",), NRC * 4096,
                           extra_writes=[rkey(c) for c in range(NDC)])
            p2_all(nheads)
        if 3 in phases:
            p3_oload(0)
            for t in range(nt3):
                p3_tile(t)

        sb.emit(es.enter_context(nc.Block()), es)
    return nc


def _swizzle(inputs):
    f32 = np.float32
    Wq = np.asarray(inputs["L0_attn_Wq"], f32)
    Wkvd = np.asarray(inputs["L0_attn_Wkv_down"], f32)
    Wkvu = np.asarray(inputs["L0_attn_Wkv_up"], f32)
    Wo = np.asarray(inputs["L0_attn_Wo"], f32)
    Wg = np.asarray(inputs["L0_ffn_Wg"], f32)
    W0a = np.asarray(inputs["L0_ffn_W0a"], f32)
    W0b = np.asarray(inputs["L0_ffn_W0b"], f32)
    W1a = np.asarray(inputs["L0_ffn_W1a"], f32)
    W1b = np.asarray(inputs["L0_ffn_W1b"], f32)
    C = np.ascontiguousarray
    d = {}
    d["wq_s"] = C(Wq.reshape(NDC, 128, H, 128).transpose(2, 1, 0, 3).reshape(H * 128, NDC * 128))
    d["wkvd_s"] = C(Wkvd.reshape(NDC, 128, NRC, 128).transpose(1, 0, 2, 3).reshape(128, NDC * NRC * 128))
    d["wkvu_s"] = C(Wkvu.reshape(NRC, 128, 4096).transpose(1, 0, 2).reshape(128, NRC * 4096))
    d["wo_s"] = C(Wo.reshape(NDC, 128, NDC, 128).transpose(2, 1, 0, 3).reshape(NDC * 128, NDC * 128))
    d["wg_s"] = C(Wg.T.reshape(NDC, 128, 2).transpose(1, 0, 2).reshape(128, NDC * 2))
    d["w0a_s"] = C(W0a.reshape(NFC, 128, NDC, 128).transpose(0, 3, 2, 1).reshape(NFC * 128, NDC * 128))
    d["w1a_s"] = C(W1a.reshape(NFC, 128, NDC, 128).transpose(0, 3, 2, 1).reshape(NFC * 128, NDC * 128))
    d["w0b_s"] = C(W0b.reshape(NDC, 128, NFC, 128).transpose(0, 3, 2, 1).reshape(NDC * 128, NFC * 128))
    d["w1b_s"] = C(W1b.reshape(NDC, 128, NFC, 128).transpose(0, 3, 2, 1).reshape(NDC * 128, NFC * 128))
    d["b0a_s"] = C(np.asarray(inputs["L0_ffn_b0a"], f32).reshape(NFC, 128).T)
    d["b1a_s"] = C(np.asarray(inputs["L0_ffn_b1a"], f32).reshape(NFC, 128).T)
    d["b0b_s"] = C(np.asarray(inputs["L0_ffn_b0b"], f32).reshape(NDC, 128).T)
    d["b1b_s"] = C(np.asarray(inputs["L0_ffn_b1b"], f32).reshape(NDC, 128).T)
    d["kvw_s"] = C(np.asarray(inputs["L0_attn_kv_norm_w"], f32).reshape(NRC, 128).T)
    d["id_s"] = np.eye(128, dtype=f32)
    k = np.arange(128)[:, None]
    q = np.arange(512)[None, :]
    m = np.concatenate([(q >= jj * 128 + k) for jj in range(4)], axis=1)
    d["mask_s"] = m.astype(ml_dtypes.bfloat16)
    return d


def kernel(**inputs):
    x = np.asarray(inputs["L0_attn_x"], np.float32)
    wz = _swizzle(inputs)
    nc = build_nc(debug=False)
    in_maps = []
    for b in range(8):
        m = dict(wz)
        m["x"] = np.ascontiguousarray(x[b])
        in_maps.append(m)
    res = run_bass_kernel_spmd(nc, in_maps, core_ids=list(range(8)))
    out = np.stack([r["y"] for r in res.results], axis=0)
    return out


# version 27
# speedup vs baseline: 1.0523x; 1.0523x over previous
import numpy as np
import ml_dtypes
import concourse.bass as bass
import concourse.mybir as mybir
from concourse.bass_utils import run_bass_kernel_spmd
from contextlib import ExitStack

F32 = mybir.dt.float32
BF16 = mybir.dt.bfloat16
AF = mybir.ActivationFunctionType
ALU = mybir.AluOpType

L = 4096
D = 2048
NDC = D // 128
H = 16
DH = 128
RK = 256
NRC = RK // 128
FF = 4096
NFC = FF // 128
T = 512
NT = L // T
NG = T // 128
NKC = L // 128
EPS_LN = 1e-5
EPS_RMS = 1e-6
QSCALE = 1.0 / np.sqrt(128.0)

P1_STAGES = ("ln1", "ffn", "ln2", "latent", "q")
COMPUTE_ENGINES = ("tensor", "vector", "scalar", "gpsimd")
ALL_ENGINES = COMPUTE_ENGINES + ("sync",)


class SB:

    def __init__(self, nc):
        self.nc = nc
        self.ops = []
        self.lastw = {}
        self.readers = {}

    def add(self, eng, fn, reads=(), writes=(), deps=(), dma=False, tag=""):
        dep = set(deps)
        for r in reads:
            w = self.lastw.get(r)
            if w is not None:
                dep.add(w)
        i = len(self.ops)
        for r in reads:
            self.readers.setdefault(r, set()).add(i)
        for w in writes:
            lw = self.lastw.get(w)
            if lw is not None:
                dep.add(lw)
            dep.update(x for x in self.readers.get(w, ()) if x != i)
        self.ops.append(dict(eng=eng, fn=fn, deps=dep, dma=dma, tag=tag))
        for w in writes:
            self.lastw[w] = i
            self.readers[w] = set()
        return i

    def dma(self, eng, fn, reads=(), writes=(), deps=(), tag=""):
        return self.add(eng, fn, reads, writes, deps, dma=True, tag=tag)

    def emit(self, block, es):
        nc = self.nc
        n = len(self.ops)
        csem = {e: es.enter_context(nc.semaphore(f"c_{e}")) for e in COMPUTE_ENGINES}
        qsem = {e: es.enter_context(nc.semaphore(f"q_{e}")) for e in ALL_ENGINES}

        filt = [None] * n
        needed = [False] * n
        for i, op in enumerate(self.ops):
            keep = []
            for d in op["deps"]:
                p = self.ops[d]
                if p["dma"]:
                    if op["dma"] and p["eng"] == op["eng"]:
                        continue
                    keep.append(d)
                else:
                    if p["eng"] == op["eng"] and not op["dma"]:
                        if p["eng"] == "tensor":
                            continue
                        keep.append(d)
                    else:
                        keep.append(d)
            filt[i] = keep
            for d in keep:
                needed[d] = True
        for i, op in enumerate(self.ops):
            if op["dma"]:
                needed[i] = True
        val = [0] * n
        counts = {}
        for i, op in enumerate(self.ops):
            if needed[i]:
                if op["dma"]:
                    key = ("q", op["eng"])
                    counts[key] = counts.get(key, 0) + 16
                else:
                    key = ("c", op["eng"])
                    counts[key] = counts.get(key, 0) + 1
                val[i] = counts[key]
        totals = dict(counts)

        semof = lambda i: (qsem[self.ops[i]["eng"]] if self.ops[i]["dma"]
                           else csem[self.ops[i]["eng"]])
        semkey = lambda i: (("q", self.ops[i]["eng"]) if self.ops[i]["dma"]
                            else ("c", self.ops[i]["eng"]))

        def run_engine(e, eng_name):
            waited = {}
            for i, op in enumerate(self.ops):
                if op["eng"] != eng_name:
                    continue
                tgt = {}
                for d in filt[i]:
                    k = semkey(d)
                    tgt[k] = max(tgt.get(k, 0), val[d])
                if op["dma"]:
                    k = ("q", eng_name)
                    v = val[i] - 16
                    if v > 0:
                        tgt[k] = max(tgt.get(k, 0), v)
                for k, v in sorted(tgt.items()):
                    if waited.get(k, 0) >= v:
                        continue
                    e.wait_ge(qsem[k[1]] if k[0] == "q" else csem[k[1]], v)
                    waited[k] = v
                inst = op["fn"](e)
                if needed[i]:
                    inst.then_inc(semof(i), 16 if op["dma"] else 1)
            if eng_name == "sync":
                for k, v in sorted(totals.items()):
                    if waited.get(k, 0) >= v:
                        continue
                    e.wait_ge(qsem[k[1]] if k[0] == "q" else csem[k[1]], v)

        for name in ALL_ENGINES:
            dec = getattr(block, {"tensor": "tensor", "vector": "vector",
                                  "scalar": "scalar", "gpsimd": "gpsimd",
                                  "sync": "sync"}[name])
            dec(lambda e, _n=name: run_engine(e, _n))


def build_nc(debug=False, phases=(0, 1, 2, 3), nt1=NT, nheads=H, nt3=NT, detect_races=True):
    nc = bass.Bass("TRN2", target_bir_lowering=False, detect_race_conditions=detect_races)

    DI = lambda name, shape, dt=F32: nc.dram_tensor(name, shape, dt, kind="ExternalInput").ap()
    x_in = DI("x", [L, D])
    wq_s = DI("wq_s", [H * 128, NDC * 128])
    wkvd_s = DI("wkvd_s", [128, NDC * NRC * 128])
    wkvu_s = DI("wkvu_s", [128, NRC * 4096])
    wo_s = DI("wo_s", [NDC * 128, NDC * 128])
    wg_s = DI("wg_s", [128, NDC * 2])
    w0a_s = DI("w0a_s", [NFC * 128, NDC * 128])
    w1a_s = DI("w1a_s", [NFC * 128, NDC * 128])
    w0b_s = DI("w0b_s", [NDC * 128, NFC * 128])
    w1b_s = DI("w1b_s", [NDC * 128, NFC * 128])
    b0a_s = DI("b0a_s", [128, NFC])
    b1a_s = DI("b1a_s", [128, NFC])
    b0b_s = DI("b0b_s", [128, NDC])
    b1b_s = DI("b1b_s", [128, NDC])
    kvw_s = DI("kvw_s", [128, NRC])
    id_s = DI("id_s", [128, 128])
    mask_s = DI("mask_s", [128, 4 * 512], BF16)

    y_out = nc.dram_tensor("y", [L, D], F32, kind="ExternalOutput").ap()

    SCR = "ExternalOutput" if debug else "Internal"
    wq_b = nc.dram_tensor("wq_b", [H * 128, NDC * 128], BF16, kind="Internal").ap()
    wo_b = nc.dram_tensor("wo_b", [NDC * 128, NDC * 128], BF16, kind="Internal").ap()
    w0a_b = nc.dram_tensor("w0a_b", [NFC * 128, NDC * 128], BF16, kind="Internal").ap()
    w1a_b = nc.dram_tensor("w1a_b", [NFC * 128, NDC * 128], BF16, kind="Internal").ap()
    w0b_b = nc.dram_tensor("w0b_b", [NDC * 128, NFC * 128], BF16, kind="Internal").ap()
    w1b_b = nc.dram_tensor("w1b_b", [NDC * 128, NFC * 128], BF16, kind="Internal").ap()
    q_scr = nc.dram_tensor("q_scr", [H * 128, L], BF16, kind=SCR).ap()
    r_scr = nc.dram_tensor("r_scr", [NDC * 128, L], BF16, kind=SCR).ap()
    o_scr = nc.dram_tensor("o_scr", [H * 128, L], BF16, kind=SCR).ap()

    es = ExitStack()
    with es:
        S = lambda name, shape, dt: es.enter_context(nc.sbuf_tensor(name, shape, dt))
        latent = S("latent", [128, NRC * L], BF16)
        wkvd = S("wkvd", [128, NDC * NRC * 128], BF16)
        wg = S("wg", [128, NDC * 2], BF16)
        b0a = S("b0a", [128, NFC], F32)
        b1a = S("b1a", [128, NFC], F32)
        b0b = S("b0b", [128, NDC], F32)
        b1b = S("b1b", [128, NDC], F32)
        kvw = S("kvw", [128, NRC], F32)
        id128 = S("id128", [128, 128], F32)
        masks = S("masks", [128, 4 * 512], BF16)
        ones_col = S("ones_col", [128, 1], BF16)
        onesk1 = S("onesk1", [1, 128], BF16)
        zeros_f = S("zeros_f", [128, 1], F32)
        epsln_f = S("epsln_f", [128, 1], F32)
        epsrms_f = S("epsrms_f", [128, 1], F32)
        A1 = S("A1", [128, 16384], BF16)
        A2 = S("A2", [128, 8192], BF16)
        A3 = S("A3", [128, 8192], BF16)
        A4 = S("A4", [128, 8192], BF16)
        A5 = S("A5", [128, 8192], BF16)
        sqb = S("sqb", [128, 1024], BF16)
        wsa = S("wsa", [128, 2 * 2048], BF16)
        wsb = S("wsb", [128, 2 * 4096], BF16)
        FA = S("FA", [128, 6144], F32)
        F2 = S("F2", [128, 3072], F32)
        st_am = S("st_am", [1, 512], F32)
        st_aq = S("st_aq", [1, 512], F32)
        st_rstd = S("st_rstd", [1, 512], F32)
        st_m2 = st_rstd
        st_d = S("st_d", [1, 512], F32)
        st_ms = st_d
        st_g = S("st_g", [1, 512], F32)
        s16 = S("s16", [1, 16], F32)
        dwg = S("dwg", [128, NDC], BF16)
        sqtab = S("sqtab", [128, H], F32)
        skvdtab = S("skvdtab", [128, NRC], F32)
        bf_am = S("bf_am", [1, 512], BF16)
        bf_rstd = S("bf_rstd", [1, 512], BF16)
        bf_mask = S("bf_mask", [1, 512], BF16)
        bf_rs = bf_am
        st_rs = st_am

        pp0 = es.enter_context(nc.psum_tensor("pp0", [128, 1024], F32))
        pp1 = es.enter_context(nc.psum_tensor("pp1", [128, 1024], F32))
        pb4 = es.enter_context(nc.psum_tensor("pb4", [128, 512], F32))
        pb5 = es.enter_context(nc.psum_tensor("pb5", [128, 512], F32))
        pb6 = es.enter_context(nc.psum_tensor("pb6", [128, 512], F32))
        pb7 = es.enter_context(nc.psum_tensor("pb7", [128, 512], F32))
        PPH = [pp0[:, 0:512], pp0[:, 512:1024]]
        PPO = [pp1[:, 0:512], pp1[:, 512:1024]]
        KH = ["pp0L", "pp0R"]
        KO = ["pp1L", "pp1R"]

        sb = SB(nc)

        def cp(dst, src, reads, writes, eng="vector", tag="cp"):
            sb.add(eng, lambda e, d=dst, s=src: e.tensor_copy(d, s), reads, writes, tag=tag)

        def convert_direct(src_ap, dst_sb, key, ncols, extra_writes=()):
            for c in range((ncols + 1023) // 1024):
                w = min(1024, ncols - c * 1024)
                par = c % 2
                stg = FA[:, 4096 + par * 1024: 4096 + par * 1024 + w]
                sb.dma("sync", lambda e, s=stg, a=src_ap[:, c * 1024: c * 1024 + w]: e.dma_start(out=s, in_=a),
                       reads=[], writes=[("cvtf", par)], tag="cvt_ld")
                sb.add("vector", lambda e, d=dst_sb[:, c * 1024: c * 1024 + w], s=stg: e.tensor_copy(d, s),
                       reads=[("cvtf", par)], writes=[key] + list(extra_writes), tag="cvt_cp")

        convq = []
        conv_done = set()
        conv_n = [0]
        pair_reg = {}
        latest_par = {}

        def conv_pair(src_ap, dst_ap, key, rr, c0):
            par = conv_n[0] % 2
            conv_n[0] += 1
            pairbuf = A5[:, 0:2048] if par == 0 else A5[:, 5120:7168]
            seq = conv_n[0] - 1
            cvtf_bufs = ((FA[:, 4096:5120], [("cvtf", 0)]),
                         (FA[:, 5120:6144], [("cvtf", 1)]),
                         (F2[:, 1536:2560], [("cvtf", 2), ("cs", 0), ("cs", 1)]))
            for half, ldq in ((0, "sync"), (1, "gpsimd")):
                slot = (2 * seq + half) % 3
                stf, skeys = cvtf_bufs[slot]
                ceng = ("gpsimd", "scalar", "vector")[slot]
                src = src_ap[rr * 128:(rr + 1) * 128, (c0 + half) * 1024:(c0 + half + 1) * 1024]
                sb.dma(ldq, lambda e, s=stf, a=src: e.dma_start(out=s, in_=a),
                       reads=[], writes=list(skeys), tag="cvt_ld")
                if ceng == "scalar":
                    sb.add("scalar", lambda e, d=pairbuf[:, half * 1024:(half + 1) * 1024], s=stf:
                           e.activation(d, s, AF.Copy, bias=0.0, scale=1.0),
                           reads=list(skeys), writes=[("cvtb", par, half)], tag="cvt_cp")
                else:
                    sb.add(ceng, lambda e, d=pairbuf[:, half * 1024:(half + 1) * 1024], s=stf:
                           e.tensor_copy(d, s),
                           reads=list(skeys), writes=[("cvtb", par, half)], tag="cvt_cp")
            sb.dma("scalar", lambda e, d=dst_ap[rr * 128:(rr + 1) * 128, c0 * 1024:(c0 + 2) * 1024], s=pairbuf:
                   e.dma_start(out=d, in_=s),
                   reads=[("cvtb", par, 0), ("cvtb", par, 1)],
                   writes=[(key, rr, c0), (key, rr, c0 + 1)], tag="cvt_st")
            conv_done.add((key, rr, c0))
            conv_done.add((key, rr, c0 + 1))
            pair_reg[(key, rr, c0)] = (par, conv_n[0])
            latest_par[par] = conv_n[0]

        def convert_scratch(src_ap, dst_ap, key, nrows, ncols):
            assert ncols % 2048 == 0
            for rr in range(nrows // 128):
                for c0 in range(0, ncols // 1024, 2):
                    convq.append(lambda rr=rr, c0=c0: conv_pair(src_ap, dst_ap, key, rr, c0))

        def pump(k):
            for _ in range(k):
                if convq:
                    convq.pop(0)()

        def need(*keys):
            while convq and not all(k in conv_done for k in keys):
                pump(1)

        def fresh_pair(key, rr, c0):
            ent = pair_reg.get((key, rr, c0))
            if ent is not None and latest_par.get(ent[0]) == ent[1]:
                return A5[:, 0:2048] if ent[0] == 0 else A5[:, 5120:7168], ent[0]
            return None

        def load_or_copy(dst, src_dram, key, rr, ncp, wkeys, direct, tag, queue="sync"):
            if direct:
                fps = [fresh_pair(key, rr, 2 * k) for k in range(ncp)]
                if all(f is not None for f in fps):
                    for k, (pb_, par_) in enumerate(fps):
                        sb.add("vector", lambda e, d=dst[:, k * 2048:(k + 1) * 2048], s=pb_:
                               e.tensor_copy(d, s),
                               reads=[("cvtb", par_, 0), ("cvtb", par_, 1)],
                               writes=list(wkeys), tag=tag + "c")
                    return
            sb.dma(queue, lambda e, d=dst, a=src_dram: e.dma_start(out=d, in_=a),
                   reads=[(key, rr, c) for c in range(2 * ncp)], writes=list(wkeys), tag=tag)

        def prelude():
            sb.add("vector", lambda e: e.memset(ones_col[:], 1.0), writes=[("ones",)])
            sb.add("vector", lambda e: e.memset(onesk1[:], 1.0), writes=[("onesk1",)])
            sb.add("vector", lambda e: e.memset(zeros_f[:], 0.0), writes=[("zeros",)])
            sb.add("vector", lambda e: e.memset(epsln_f[:], EPS_LN), writes=[("epsln",)])
            sb.add("vector", lambda e: e.memset(epsrms_f[:], EPS_RMS), writes=[("epsrms",)])
            for (ap_, sbuf, key) in ((id_s, id128, ("id",)), (mask_s, masks, ("masks",)),
                                     (b0a_s, b0a, ("b0a",)), (b1a_s, b1a, ("b1a",)),
                                     (b0b_s, b0b, ("b0b",)), (b1b_s, b1b, ("b1b",)),
                                     (kvw_s, kvw, ("kvw",))):
                sb.dma("gpsimd", lambda e, d=sbuf, s=ap_: e.dma_start(out=d[:], in_=s), writes=[key], tag="cst_ld")
            convert_direct(wg_s, wg, ("wg",), NDC * 2)
            convert_direct(wkvd_s, wkvd, ("wkvd",), NDC * NRC * 128)
            for c in range(NDC):
                sb.add("vector", lambda e, c=c: e.tensor_tensor(dwg[:, c:c + 1], wg[:, 2 * c:2 * c + 1],
                                                                wg[:, 2 * c + 1:2 * c + 2], ALU.subtract),
                       reads=[("wg",)], writes=[("dwg",)], tag="dwg")
            sb.add("tensor", lambda e: e.matmul(pb6[0:1, 0:NDC], ones_col[:], dwg[:], start=True, stop=True),
                   reads=[("dwg",), ("ones",)], writes=["pb6"], tag="dS_mm")
            sb.add("vector", lambda e: e.tensor_copy(s16[:], pb6[0:1, 0:NDC]),
                   reads=["pb6"], writes=["s16"], tag="dS_ev")
            for hw in (8, 4, 2, 1):
                sb.add("vector", lambda e, hw=hw: e.tensor_tensor(s16[:, 0:hw], s16[:, 0:hw], s16[:, hw:2 * hw], ALU.add),
                       reads=["s16"], writes=["s16"], tag="dS_add")
            for k in range(8):
                sb.add("tensor", lambda e, k=k: e.matmul(pb4[0:1, :], ones_col[:], wkvd[:, k * 512:(k + 1) * 512],
                                                         start=(k == 0), stop=(k == 7)),
                       reads=[("wkvd",), ("ones",)], writes=["pb4"], tag="skvd_mm")
            sb.add("vector", lambda e: e.tensor_copy(st_g[:], pb4[0:1, :]), reads=["pb4"], writes=["st_g"], tag="skvd_ev")
            sb.add("vector", lambda e: e.tensor_tensor(st_g[:, 0:128], st_g[:, 0:128], st_g[:, 256:384], ALU.add),
                   reads=["st_g"], writes=["st_g"], tag="skvd_f0")
            sb.add("vector", lambda e: e.tensor_tensor(st_g[:, 128:256], st_g[:, 128:256], st_g[:, 384:512], ALU.add),
                   reads=["st_g"], writes=["st_g"], tag="skvd_f1")
            sb.add("vector", lambda e: e.tensor_copy(bf_am[:, 0:256], st_g[:, 0:256]),
                   reads=["st_g"], writes=["bf_am"], tag="skvd_bf")
            for rc in range(NRC):
                sb.add("tensor", lambda e, rc=rc: e.matmul(pb5[:, rc:rc + 1], bf_am[:, rc * 128:(rc + 1) * 128],
                                                           onesk1[:, 0:1], start=True, stop=True),
                       reads=["bf_am", ("onesk1",)], writes=["pb5"], tag="skvd_tr")
            sb.add("vector", lambda e: e.tensor_copy(skvdtab[:], pb5[:, 0:NRC]),
                   reads=["pb5"], writes=["skvdtab"], tag="skvd_te")
            convert_scratch(w0a_s, w0a_b, "w0a_b", NFC * 128, NDC * 128)
            convert_scratch(w0b_s, w0b_b, "w0b_b", NDC * 128, NFC * 128)
            convert_scratch(w1a_s, w1a_b, "w1a_b", NFC * 128, NDC * 128)
            convert_scratch(w1b_s, w1b_b, "w1b_b", NDC * 128, NFC * 128)
            convert_scratch(wq_s, wq_b, "wq_b", H * 128, NDC * 128)
            convert_scratch(wo_s, wo_b, "wo_b", NDC * 128, NDC * 128)
            pump(8)

        def ln_stats_chunk(c, rkeys, rview, tag, with_gate=False, sq_eng="scalar"):
            par = c % 2
            sq = sqb[:, par * 512:(par + 1) * 512]
            sb.add("tensor", lambda e, c=c: e.matmul(pb4[0:1, :], ones_col[:], rview(c),
                                                     start=(c == 0), stop=(c == NDC - 1)),
                   reads=[rkeys(c), ("ones",)], writes=["pb4"], tag=f"{tag}_mean")
            if sq_eng == "scalar":
                sb.add("scalar", lambda e, s=sq, c=c: e.activation(s, rview(c), AF.Square, bias=0.0, scale=1.0),
                       reads=[rkeys(c)], writes=[("sq", par)], tag=f"{tag}_sq")
            else:
                sb.add("gpsimd", lambda e, s=sq, c=c: e.tensor_tensor(s, rview(c), rview(c), ALU.mult),
                       reads=[rkeys(c)], writes=[("sq", par)], tag=f"{tag}_sq")
            sb.add("tensor", lambda e, s=sq, c=c: e.matmul(pb5[0:1, :], ones_col[:], s,
                                                           start=(c == 0), stop=(c == NDC - 1)),
                   reads=[("sq", par), ("ones",)], writes=["pb5"], tag=f"{tag}_sumsq")
            if with_gate:
                sb.add("tensor", lambda e, c=c: e.matmul(PPO[0][0:1, :], dwg[:, c:c + 1], rview(c),
                                                         start=(c == 0), stop=(c == NDC - 1)),
                       reads=[rkeys(c), ("dwg",)], writes=[KO[0]], tag=f"{tag}_gmm")

        def ln_finish(rkeys, rview, xnview, xnkeys, tag, make_xn=True):
            sb.add("vector", lambda e: e.tensor_scalar(st_am[:], pb4[0:1, :], 1.0 / D, None, ALU.mult),
                   reads=["pb4"], writes=["st_am"], tag=f"{tag}_am")
            sb.add("vector", lambda e: e.tensor_scalar(st_aq[:], pb5[0:1, :], 1.0 / D, None, ALU.mult),
                   reads=["pb5"], writes=["st_aq"], tag=f"{tag}_aq")
            sb.add("vector", lambda e: e.tensor_tensor(st_m2[:], st_am[:], st_am[:], ALU.mult),
                   reads=["st_am"], writes=["st_rstd"], tag=f"{tag}_m2")
            sb.add("vector", lambda e: e.tensor_tensor(st_aq[:], st_aq[:], st_m2[:], ALU.subtract),
                   reads=["st_aq", "st_rstd"], writes=["st_aq"], tag=f"{tag}_var")
            sb.add("scalar", lambda e: e.activation(st_rstd[:], st_aq[:], AF.Ln, bias=epsln_f[0:1, :], scale=1.0),
                   reads=["st_aq", ("epsln",)], writes=["st_rstd"], tag=f"{tag}_ln")
            sb.add("scalar", lambda e: e.activation(st_rstd[:], st_rstd[:], AF.Exp, bias=zeros_f[0:1, :], scale=-0.5),
                   reads=["st_rstd", ("zeros",)], writes=["st_rstd"], tag=f"{tag}_exp")
            sb.add("vector", lambda e: e.tensor_copy(bf_am[:], st_am[:]),
                   reads=["st_am"], writes=["bf_am"], tag=f"{tag}_ambf")
            sb.add("vector", lambda e: e.tensor_copy(bf_rstd[:], st_rstd[:]),
                   reads=["st_rstd"], writes=["bf_rstd"], tag=f"{tag}_rsbf")
            sb.add("tensor", lambda e: e.matmul(pb4[:], onesk1[:], bf_am[:], start=True, stop=True),
                   reads=["bf_am", ("onesk1",)], writes=["pb4"], tag=f"{tag}_bcm")
            sb.add("tensor", lambda e: e.matmul(pb5[:], onesk1[:], bf_rstd[:], start=True, stop=True),
                   reads=["bf_rstd", ("onesk1",)], writes=["pb5"], tag=f"{tag}_bcr")
            mb = F2[:, 0:512]
            rb = F2[:, 512:1024]
            sb.add("vector", lambda e: e.tensor_copy(mb, pb4[:]), reads=["pb4"], writes=["mb"], tag=f"{tag}_mbe")
            sb.add("vector", lambda e: e.tensor_copy(rb, pb5[:]), reads=["pb5"], writes=["rb"], tag=f"{tag}_rbe")
            if not make_xn:
                return
            for c in range(NDC - 4, NDC):
                tmp = sqb[:, (c % 2) * 512:((c % 2) + 1) * 512]
                sb.add("gpsimd", lambda e, t=tmp, c=c: e.tensor_tensor(t, rview(c), mb, ALU.subtract),
                       reads=[rkeys(c), "mb"], writes=[("sq", c % 2)], tag=f"{tag}_sub")
                sb.add("gpsimd", lambda e, t=tmp, c=c: e.tensor_tensor(xnview(c), t, rb, ALU.mult),
                       reads=[("sq", c % 2), "rb"], writes=[xnkeys(c)], tag=f"{tag}_mul")
            for c in range(NDC - 4):
                tmp = F2[:, 1536 + (c % 2) * 512: 1536 + (c % 2 + 1) * 512]
                sb.add("vector", lambda e, t=tmp, c=c: e.tensor_tensor(t, rview(c), mb, ALU.subtract),
                       reads=[rkeys(c), "mb"], writes=[("cs", c % 2)], tag=f"{tag}_sub")
                sb.add("vector", lambda e, t=tmp, c=c: e.tensor_tensor(xnview(c), t, rb, ALU.mult),
                       reads=[("cs", c % 2), "rb"], writes=[xnkeys(c)], tag=f"{tag}_mul")

        def gate_combine(tag):
            sb.add("vector", lambda e: e.tensor_copy(st_g[:], PPO[0][0:1, :]),
                   reads=[KO[0]], writes=["st_g"], tag=f"{tag}_gev")
            sb.add("vector", lambda e: e.tensor_scalar(st_d[:], st_am[:], s16[:, 0:1], None, ALU.mult),
                   reads=["st_am", "s16"], writes=["st_d"], tag=f"{tag}_gms")
            sb.add("vector", lambda e: e.tensor_tensor(st_d[:], st_g[:], st_d[:], ALU.subtract),
                   reads=["st_g", "st_d"], writes=["st_d"], tag=f"{tag}_gd")
            sb.add("vector", lambda e: e.tensor_scalar(st_d[:], st_d[:], 0.0, None, ALU.is_ge),
                   reads=["st_d"], writes=["st_d"], tag=f"{tag}_gm")
            sb.add("vector", lambda e: e.tensor_copy(bf_mask[:], st_d[:]),
                   reads=["st_d"], writes=["bf_mask"], tag=f"{tag}_gmb")
            sb.add("tensor", lambda e: e.matmul(pb6[:], onesk1[:], bf_mask[:], start=True, stop=True),
                   reads=["bf_mask", ("onesk1",)], writes=["pb6"], tag=f"{tag}_gbc")
            maskb = F2[:, 1024:1536]
            sb.add("vector", lambda e: e.tensor_copy(maskb, pb6[:]), reads=["pb6"], writes=["maskb"], tag=f"{tag}_gbe")

        def ffn(xnview, xnkeys, tag, out_cb, direct=False):
            GB = [(PPH[0], KH[0]), (PPH[1], KH[1]), (PPO[0], KO[0]), (PPO[1], KO[1]),
                  (pb4[:], "pb4"), (pb5[:], "pb5")]
            for eidx in (0, 1):
                wa_b = w0a_b if eidx == 0 else w1a_b
                wb_b = w0b_b if eidx == 0 else w1b_b
                ba = b0a if eidx == 0 else b1a
                bb = b0b if eidx == 0 else b1b
                ba_key = ("b0a",) if eidx == 0 else ("b1a",)
                bb_key = ("b0b",) if eidx == 0 else ("b1b",)
                wa_key = "w0a_b" if eidx == 0 else "w1a_b"
                wb_key = "w0b_b" if eidx == 0 else "w1b_b"
                NG0 = 6 if eidx == 0 else 0
                if NG0:
                    gw = [wsa[:, 0:2048], wsa[:, 2048:4096],
                          A4[:, 0:2048], A4[:, 2048:4096], A4[:, 4096:6144], A4[:, 6144:8192]]
                    for j in range(NG0):
                        extra = [("o0", 4 * (j - 2)), ("o0", 4 * (j - 2) + 1), ("o0", 4 * (j - 2) + 2),
                                 ("o0", 4 * (j - 2) + 3)] if j >= 2 else []
                        wkey = ("wsa", j) if j < 2 else ("wsa2", j)
                        need((wa_key, j, 0), (wa_key, j, 1))
                        load_or_copy(gw[j], wa_b[j * 128:(j + 1) * 128, :], wa_key, j, 1,
                                     [wkey] + extra, direct, f"{tag}_wal")
                        pump(2)
                    for c in range(NDC):
                        for j in range(NG0):
                            pb_, pk = GB[j]
                            wkey = ("wsa", j) if j < 2 else ("wsa2", j)
                            sb.add("tensor", lambda e, c=c, j=j, pb_=pb_:
                                   e.matmul(pb_, gw[j][:, c * 128:(c + 1) * 128],
                                            xnview(c), start=(c == 0), stop=(c == NDC - 1)),
                                   reads=[wkey, xnkeys(c)], writes=[pk], tag=f"{tag}_h{eidx}")
                    for j in range(NG0):
                        pb_, pk = GB[j]
                        hv = A1[:, j * 512:(j + 1) * 512]
                        sb.add("scalar", lambda e, h=hv, j=j, pb_=pb_: e.activation(h, pb_, AF.Gelu,
                                                                                   bias=ba[:, j:j + 1], scale=1.0),
                               reads=[pk, ba_key], writes=[("h", j)], tag=f"{tag}_ge")
                for j in range(NG0, NFC):
                    par = j % 2
                    wbuf = wsa[:, par * 2048:(par + 1) * 2048]
                    need((wa_key, j, 0), (wa_key, j, 1))
                    load_or_copy(wbuf, wa_b[j * 128:(j + 1) * 128, :], wa_key, j, 1,
                                 [("wsa", par)], direct, f"{tag}_wal")
                    pump(2)
                    for c in range(NDC):
                        sb.add("tensor", lambda e, c=c, par=par, j=j:
                               e.matmul(PPH[j % 2], wsa[:, (j % 2) * 2048 + c * 128:(j % 2) * 2048 + (c + 1) * 128],
                                        xnview(c), start=(c == 0), stop=(c == NDC - 1)),
                               reads=[("wsa", par), xnkeys(c)], writes=[KH[j % 2]], tag=f"{tag}_h{eidx}")
                    hv = A1[:, j * 512:(j + 1) * 512]
                    if eidx == 0:
                        sb.add("scalar", lambda e, h=hv, j=j: e.activation(h, PPH[j % 2], AF.Gelu,
                                                                           bias=ba[:, j:j + 1], scale=1.0),
                               reads=[KH[j % 2], ba_key], writes=[("h", j)], tag=f"{tag}_ge")
                    else:
                        sb.add("vector", lambda e, h=hv, j=j: e.tensor_scalar(h, PPH[j % 2], ba[:, j:j + 1], 0.0,
                                                                              ALU.add, ALU.max),
                               reads=[KH[j % 2], ba_key], writes=[("h", j)], tag=f"{tag}_re")
                for i in range(NDC):
                    par = i % 2
                    wbuf = wsb[:, par * 4096:(par + 1) * 4096]
                    need((wb_key, i, 0), (wb_key, i, 1), (wb_key, i, 2), (wb_key, i, 3))
                    load_or_copy(wbuf, wb_b[i * 128:(i + 1) * 128, :], wb_key, i, 2,
                                 [("wsb", par)], direct, f"{tag}_wbl", queue="scalar")
                    pump(3)
                    for j in range(NFC):
                        sb.add("tensor", lambda e, i=i, j=j, par=par:
                               e.matmul(PPO[i % 2], wsb[:, par * 4096 + j * 128: par * 4096 + (j + 1) * 128],
                                        A1[:, j * 512:(j + 1) * 512], start=(j == 0), stop=(j == NFC - 1)),
                               reads=[("wsb", par), ("h", j)], writes=[KO[i % 2]], tag=f"{tag}_o{eidx}")
                    if eidx == 0:
                        ov = A4[:, i * 512:(i + 1) * 512]
                        sb.add("vector", lambda e, o=ov, i=i: e.tensor_scalar(o, PPO[i % 2], bb[:, i:i + 1], 0.5,
                                                                              ALU.add, ALU.mult),
                               reads=[KO[i % 2], bb_key], writes=[("o0", i)], tag=f"{tag}_oe0")
                    else:
                        out_cb(i, i % 2, bb)

        rview = lambda c: A3[:, c * 512:(c + 1) * 512]
        rkey = lambda c: ("r", c)
        xnview = lambda c: A2[:, c * 512:(c + 1) * 512]
        xnkey = lambda c: ("xn", c)

        def p1_xload(t, g):
            par = g % 2
            xin = FA[:, par * 2048:(par + 1) * 2048]
            sb.dma(("gpsimd", "sync")[par], lambda e, d=xin, a=x_in[t * T + g * 128: t * T + (g + 1) * 128, :]:
                   e.dma_start(out=d, in_=a), writes=[("xin", par)], tag="p1_xld")

        def p1_tile(t):
            stages = P1_STAGES
            for g in range(NG):
                par = g % 2
                xin = FA[:, par * 2048:(par + 1) * 2048]
                if g >= 2:
                    p1_xload(t, g)
                pump(1)
                for c in range(NDC):
                    pbt = pb6 if c % 2 == 0 else pb7
                    pkey = "pb6" if c % 2 == 0 else "pb7"
                    sb.add("tensor", lambda e, s=xin[:, c * 128:(c + 1) * 128], p=pbt:
                           e.transpose(p[:, 0:128], s, id128[:]),
                           reads=[("xin", par), ("id",)], writes=[pkey], tag="p1_xtr")
                    sb.add("vector", lambda e, p=pbt, d=rview(c)[:, g * 128:(g + 1) * 128]:
                           e.tensor_copy(d, p[:, 0:128]),
                           reads=[pkey], writes=[rkey(c)], tag="p1_xev")
                    if g == NG - 1 and "ln1" in stages and c > 1:
                        ln_stats_chunk(c - 2, rkey, rview, f"p1ln_{t}", with_gate=True)
            if "ln1" not in stages:
                return
            ln_stats_chunk(NDC - 2, rkey, rview, f"p1ln_{t}", with_gate=True)
            ln_stats_chunk(NDC - 1, rkey, rview, f"p1ln_{t}", with_gate=True)
            ln_finish(rkey, rview, xnview, xnkey, f"p1ln_{t}")
            gate_combine(f"p1g_{t}")

            def p1_out(i, par, bb):
                o1t = A5[:, 2048 + par * 512: 2048 + (par + 1) * 512]
                dt_ = A5[:, 3072:3584] if par == 0 else A5[:, 3584:4096]
                sb.add("vector", lambda e, o=o1t, i=i, par=par:
                       e.tensor_scalar(o, PPO[par], bb[:, i:i + 1], 0.5, ALU.add, ALU.mult),
                       reads=[KO[par], ("b1b",)], writes=[("o1t", par)], tag="p1_oe1")
                sb.add("vector", lambda e, d=dt_, o=o1t, i=i:
                       e.tensor_tensor(d, A4[:, i * 512:(i + 1) * 512], o, ALU.subtract),
                       reads=[("o1t", par), ("o0", i)], writes=[("dt", par)], tag="p1_df")
                sb.add("vector", lambda e, d=dt_: e.tensor_tensor(d, d, F2[:, 1024:1536], ALU.mult),
                       reads=[("dt", par), "maskb"], writes=[("dt", par)], tag="p1_sel")
                sb.add("vector", lambda e, d=dt_, o=o1t: e.tensor_tensor(d, d, o, ALU.add),
                       reads=[("dt", par), ("o1t", par)], writes=[("dt", par)], tag="p1_cmb")
                sb.add("vector", lambda e, d=dt_, i=i: e.tensor_tensor(rview(i), rview(i), d, ALU.add),
                       reads=[("dt", par), rkey(i)], writes=[rkey(i)], tag="p1_radd")
                if "ln2" in stages and i > 0:
                    ln_stats_chunk(i - 1, rkey, rview, f"p1ln2_{t}", sq_eng="gpsimd")
                if i % 4 == 3:
                    c0 = i - 3
                    sb.dma("gpsimd", lambda e, c0=c0, t=t: e.dma_start(
                        out=r_scr[c0 * 128:(c0 + 4) * 128, t * T:(t + 1) * T].rearrange('(a p) w -> p a w', p=128),
                        in_=A3[:, c0 * 512:(c0 + 4) * 512].rearrange('p (a w) -> p a w', a=4)),
                        reads=[rkey(c0 + k) for k in range(4)],
                        writes=[("r_scr", c0 + k, t) for k in range(4)], tag="p1_rst")

            if "ffn" in stages:
                ffn(xnview, xnkey, f"p1f_{t}", p1_out)
            if t + 1 < NT:
                p1_xload(t + 1, 0)
                p1_xload(t + 1, 1)
            if "ln2" not in stages:
                return
            for hh in (0, 1):
                need(("wq_b", hh, 0), ("wq_b", hh, 1))
                load_or_copy(wsa[:, (hh % 2) * 2048:((hh % 2) + 1) * 2048], wq_b[hh * 128:(hh + 1) * 128, :],
                             "wq_b", hh, 1, [("wsa", hh % 2)], t == 0, "p1_wql", queue="scalar")
            ln_stats_chunk(NDC - 1, rkey, rview, f"p1ln2_{t}", sq_eng="gpsimd")
            ln_finish(rkey, rview, xnview, xnkey, f"p1ln2_{t}", make_xn=False)
            rbq = F2[:, 1024:1536]
            sb.add("vector", lambda e: e.tensor_scalar(rbq, F2[:, 512:1024], QSCALE, None, ALU.mult),
                   reads=["rb"], writes=["maskb"], tag="p1_rbq")
            if "latent" not in stages:
                return
            for rc in range(NRC):
                pbt, pkey = (pb4, "pb4") if rc == 0 else (pb5, "pb5")
                for c in range(NDC):
                    sb.add("tensor", lambda e, rc=rc, c=c, pbt=pbt:
                           e.matmul(pbt[:], wkvd[:, (c * NRC + rc) * 128:(c * NRC + rc + 1) * 128], rview(c),
                                    start=(c == 0), stop=(c == NDC - 1)),
                           reads=[("wkvd",), rkey(c)], writes=[pkey], tag="p1_lat")
                corr = F2[:, 2560:3072]
                sb.add("vector", lambda e, rc=rc, corr=corr:
                       e.tensor_scalar(corr, F2[:, 0:512], skvdtab[:, rc:rc + 1], None, ALU.mult),
                       reads=["mb", "skvdtab"], writes=["rb2"], tag="p1_lcor")
                cs = F2[:, 1536 + rc * 512: 1536 + (rc + 1) * 512]
                sb.add("vector", lambda e, d=cs, p=pbt, corr=corr: e.tensor_tensor(d, p[:], corr, ALU.subtract),
                       reads=[pkey, "rb2"], writes=[("cs", rc)], tag="p1_cse")
                sq = sqb[:, (rc % 2) * 512:((rc % 2) + 1) * 512]
                sb.add("gpsimd", lambda e, s=sq, c_=cs: e.tensor_tensor(s, c_, c_, ALU.mult),
                       reads=[("cs", rc)], writes=[("sq", rc % 2)], tag="p1_csq")
                sb.add("tensor", lambda e, s=sq, rc=rc: e.matmul(pb6[0:1, :], ones_col[:], s,
                                                                 start=(rc == 0), stop=(rc == NRC - 1)),
                       reads=[("sq", rc % 2), ("ones",)], writes=["pb6"], tag="p1_mssum")
            sb.add("vector", lambda e: e.tensor_scalar(st_ms[:], pb6[0:1, :], 1.0 / RK, None, ALU.mult),
                   reads=["pb6"], writes=["st_d"], tag="p1_ms")
            sb.add("scalar", lambda e: e.activation(st_ms[:], st_ms[:], AF.Ln, bias=epsrms_f[0:1, :], scale=1.0),
                   reads=["st_d", ("epsrms",)], writes=["st_d"], tag="p1_msl")
            sb.add("scalar", lambda e: e.activation(st_ms[:], st_ms[:], AF.Exp, bias=zeros_f[0:1, :], scale=-0.5),
                   reads=["st_d", ("zeros",)], writes=["st_d"], tag="p1_mse")
            sb.add("vector", lambda e: e.tensor_copy(bf_rstd[:], st_ms[:]),
                   reads=["st_d"], writes=["bf_rstd"], tag="p1_msb")
            sb.add("tensor", lambda e: e.matmul(pb7[:], onesk1[:], bf_rstd[:], start=True, stop=True),
                   reads=["bf_rstd", ("onesk1",)], writes=["pb7"], tag="p1_msbc")
            rb2 = F2[:, 2560:3072]
            sb.add("vector", lambda e: e.tensor_copy(rb2, pb7[:]), reads=["pb7"], writes=["rb2"], tag="p1_rb2")
            for rc in range(NRC):
                cs = F2[:, 1536 + rc * 512: 1536 + (rc + 1) * 512]
                sb.add("vector", lambda e, c_=cs: e.tensor_tensor(c_, c_, rb2, ALU.mult),
                       reads=[("cs", rc), "rb2"], writes=[("cs", rc)], tag="p1_lrm")
                sb.add("scalar", lambda e, c_=cs, rc=rc, t=t:
                       e.activation(latent[:, rc * L + t * T: rc * L + (t + 1) * T], c_, AF.Copy,
                                    scale=kvw[:, rc:rc + 1]),
                       reads=[("cs", rc), ("kvw",)], writes=[("latent", rc, t)], tag="p1_lw")
            if "q" not in stages:
                return
            for hh in range(H):
                par = hh % 2
                if hh >= 2:
                    need(("wq_b", hh, 0), ("wq_b", hh, 1))
                    load_or_copy(wsa[:, par * 2048:(par + 1) * 2048], wq_b[hh * 128:(hh + 1) * 128, :],
                                 "wq_b", hh, 1, [("wsa", par)], t == 0, "p1_wql", queue="scalar")
                pump(2)
                if t == 0:
                    for k in range(4):
                        sb.add("tensor", lambda e, k=k, par=par:
                               e.matmul(pb4[0:1, :], ones_col[:], wsa[:, par * 2048 + k * 512: par * 2048 + (k + 1) * 512],
                                        start=(k == 0), stop=(k == 3)),
                               reads=[("wsa", par), ("ones",)], writes=["pb4"], tag="sq_mm")
                    sb.add("vector", lambda e: e.tensor_copy(st_g[:], pb4[0:1, :]),
                           reads=["pb4"], writes=["st_g"], tag="sq_ev")
                    sb.add("vector", lambda e: e.tensor_tensor(st_g[:, 0:256], st_g[:, 0:256], st_g[:, 256:512], ALU.add),
                           reads=["st_g"], writes=["st_g"], tag="sq_f0")
                    sb.add("vector", lambda e: e.tensor_tensor(st_g[:, 0:128], st_g[:, 0:128], st_g[:, 128:256], ALU.add),
                           reads=["st_g"], writes=["st_g"], tag="sq_f1")
                    sb.add("vector", lambda e: e.tensor_copy(bf_am[:, 0:128], st_g[:, 0:128]),
                           reads=["st_g"], writes=["bf_am"], tag="sq_bf")
                    sb.add("tensor", lambda e, hh=hh: e.matmul(pb5[:, hh:hh + 1], bf_am[:, 0:128],
                                                               onesk1[:, 0:1], start=True, stop=True),
                           reads=["bf_am", ("onesk1",)], writes=["pb5"], tag="sq_tr")
                    sb.add("vector", lambda e, hh=hh: e.tensor_copy(sqtab[:, hh:hh + 1], pb5[:, hh:hh + 1]),
                           reads=["pb5"], writes=["sqtab"], tag="sq_te")
                for c in range(NDC):
                    sb.add("tensor", lambda e, c=c, par=par:
                           e.matmul(PPH[par], wsa[:, par * 2048 + c * 128: par * 2048 + (c + 1) * 128],
                                    rview(c), start=(c == 0), stop=(c == NDC - 1)),
                           reads=[("wsa", par), rkey(c)], writes=[KH[par]], tag="p1_q")
                qo = A5[:, 4096 + par * 512: 4096 + (par + 1) * 512]
                tmpq = F2[:, 1536 + par * 512: 1536 + (par + 1) * 512]
                sb.add("vector", lambda e, tq=tmpq, hh=hh: e.tensor_scalar(tq, F2[:, 0:512], sqtab[:, hh:hh + 1],
                                                                           None, ALU.mult),
                       reads=["mb", "sqtab"], writes=[("cs", par)], tag="p1_qc1")
                sb.add("vector", lambda e, tq=tmpq, par=par: e.tensor_tensor(tq, PPH[par], tq, ALU.subtract),
                       reads=[KH[par], ("cs", par)], writes=[("cs", par)], tag="p1_qc2")
                sb.add("vector", lambda e, q=qo, tq=tmpq: e.tensor_tensor(q, tq, F2[:, 1024:1536], ALU.mult),
                       reads=[("cs", par), "maskb"], writes=[("qo", par)], tag="p1_qe")
                if hh % 2 == 1:
                    sb.dma("gpsimd", lambda e, hh=hh, t=t: e.dma_start(
                        out=q_scr[(hh - 1) * 128:(hh + 1) * 128, t * T:(t + 1) * T].rearrange('(a p) w -> p a w', p=128),
                        in_=A5[:, 4096:5120].rearrange('p (a w) -> p a w', a=2)),
                        reads=[("qo", 0), ("qo", 1)],
                        writes=[("q_scr", hh - 1, t), ("q_scr", hh, t)], tag="p1_qst")

        def kv_bufs(hh):
            b = hh % 3
            if b < 2:
                return A1[:, b * 8192: b * 8192 + 4096], A1[:, b * 8192 + 4096: b * 8192 + 8192]
            return A5[:, 0:4096], A5[:, 4096:8192]

        def kkey(hh, kb):
            return ("ot", kb) if hh % 3 == 2 else ("k", hh % 3, kb)

        def vkey(hh, kc):
            return ("ot", 8 + kc // 4) if hh % 3 == 2 else ("v", hh % 3, kc)

        def qbuf(hh):
            b = hh % 3
            return A2[:, b * 4096:(b + 1) * 4096] if b < 2 else A4[:, 3072:7168]

        def qkey(hh):
            return ("q", hh % 3)

        def p2_emit_q(hh):
            b = hh % 3
            alias = ([("xn", c) for c in range(b * 8, b * 8 + 8)] if b < 2
                     else [("o0", i) for i in range(6, 14)])
            sb.dma("gpsimd", lambda e, hh=hh: e.dma_start(out=qbuf(hh), in_=q_scr[hh * 128:(hh + 1) * 128, :]),
                   reads=[("q_scr", hh, t_) for t_ in range(NT)],
                   writes=[qkey(hh)] + alias, tag="p2_qld")

        def p2_emit_K(hh, kb, spar):
            kvx, _ = kv_bufs(hh)
            pst = pp0 if spar == 0 else pp1
            pk = KH[0] if spar == 0 else KO[0]
            for rc in range(NRC):
                sb.add("tensor", lambda e, rc=rc, kb=kb, hh=hh, pst=pst:
                       e.matmul(pst[:, 0:512], A3[:, rc * 4096 + hh * 128: rc * 4096 + (hh + 1) * 128],
                                latent[:, rc * L + kb * 512: rc * L + (kb + 1) * 512],
                                start=(rc == 0), stop=(rc == NRC - 1)),
                       reads=[("wkvu",), ("latent", rc, kb)], writes=[pk], tag="p2_k")
            extra = [("h", (hh % 3) * 16 + kb)] if hh % 3 < 2 else \
                ([("cvtb", 0, 0), ("cvtb", 0, 1)] if kb < 4 else [("o1t", 0), ("o1t", 1), ("dt", 0), ("dt", 1)])
            sb.add("vector", lambda e, kb=kb, kvx=kvx, pst=pst: e.tensor_copy(kvx[:, kb * 512:(kb + 1) * 512], pst[:, 0:512]),
                   reads=[pk], writes=[kkey(hh, kb)] + extra, tag="p2_ke")

        def p2_emit_V(hh, kc0, spar):
            _, vvx = kv_bufs(hh)
            pst = pp0 if spar == 0 else pp1
            pk = KH[0] if spar == 0 else KO[0]
            for j in range(4):
                kc = kc0 + j
                for rc in range(NRC):
                    sb.add("tensor", lambda e, rc=rc, kc=kc, hh=hh, pst=pst, j=j:
                           e.matmul(pst[:, j * 128:(j + 1) * 128],
                                    latent[:, rc * L + kc * 128: rc * L + (kc + 1) * 128],
                                    A3[:, rc * 4096 + 2048 + hh * 128: rc * 4096 + 2048 + (hh + 1) * 128],
                                    start=(rc == 0), stop=(rc == NRC - 1)),
                           reads=[("wkvu",), ("latent", rc, kc // 4)], writes=[pk], tag="p2_v")
            extra = [("h", (hh % 3) * 16 + 8 + kc0 // 4)] if hh % 3 < 2 else \
                ([("qo", 0), ("qo", 1)] if kc0 < 8 else [("cvtb", 1, 0), ("cvtb", 1, 1)])
            sb.add("vector", lambda e, kc0=kc0, vvx=vvx, pst=pst:
                   e.tensor_copy(vvx[:, kc0 * 128:(kc0 + 4) * 128], pst[:, 0:512]),
                   reads=[pk], writes=[vkey(hh, kc0 + j) for j in range(4)] + extra, tag="p2_ve")

        def p2_emit_S(task, spar):
            hh, qb, kp = task
            pst = pp0 if spar == 0 else pp1
            pskeys = [KH[0], KH[1]] if spar == 0 else [KO[0], KO[1]]
            kvx, _ = kv_bufs(hh)
            qv = qbuf(hh)
            qx = [("o0", 6 + qb)] if hh % 3 == 2 else []
            for half in range(2):
                kc = kp * 2 + half
                sb.add("tensor", lambda e, kc=kc, qb=qb, pst=pst, half=half, kvx=kvx, qv=qv:
                       e.matmul(pst[:, half * 512:(half + 1) * 512], kvx[:, kc * 128:(kc + 1) * 128],
                                qv[:, qb * 512:(qb + 1) * 512], start=True, stop=True),
                       reads=[kkey(hh, kc // 4), qkey(hh)] + qx, writes=[pskeys[half]], tag="p2_s")

        def p2_emit_exp(task, spar):
            hh, qb, kp = task
            pst = pp0 if spar == 0 else pp1
            pskeys = [KH[0], KH[1]] if spar == 0 else [KO[0], KO[1]]
            ptv = A4[:, spar * 1024:(spar + 1) * 1024]
            sb.add("scalar", lambda e, p=ptv, pst=pst: e.activation(p, pst[:], AF.Exp, bias=zeros_f[:], scale=1.0),
                   reads=pskeys + [("zeros",)],
                   writes=[("pt", spar), ("o0", spar * 2), ("o0", spar * 2 + 1)], tag="p2_exp")
            for half in range(2):
                kc = kp * 2 + half
                if kc >= 4 * qb:
                    jj = kc - 4 * qb
                    sb.add("vector", lambda e, p=ptv, half=half, jj=jj:
                           e.tensor_tensor(p[:, half * 512:(half + 1) * 512],
                                           p[:, half * 512:(half + 1) * 512],
                                           masks[:, jj * 512:(jj + 1) * 512], ALU.mult),
                           reads=[("pt", spar), ("masks",)], writes=[("pt", spar)], tag="p2_msk")

        def p2_emit_consume(task, spar):
            hh, qb, kp = task
            par = hh % 2
            kcmax = 4 * (qb + 1)
            _, vvx = kv_bufs(hh)
            ptv = A4[:, spar * 1024:(spar + 1) * 1024]
            pvb, pvkey = (pb4, "pb4") if par == 0 else (pb7, "pb7")
            for half in range(2):
                kc = kp * 2 + half
                sb.add("tensor", lambda e, kc=kc, p=ptv, half=half, km=kcmax, pvb=pvb, vvx=vvx:
                       e.matmul(pvb[:], vvx[:, kc * 128:(kc + 1) * 128],
                                p[:, half * 512:(half + 1) * 512],
                                start=(kc == 0), stop=(kc == km - 1)),
                       reads=[vkey(hh, kc), ("pt", spar)], writes=[pvkey], tag="p2_pv")
                sb.add("tensor", lambda e, kc=kc, p=ptv, half=half, km=kcmax, par=par:
                       e.matmul((pb5 if par == 0 else pb6)[0:1, :], ones_col[:],
                                p[:, half * 512:(half + 1) * 512],
                                start=(kc == 0), stop=(kc == km - 1)),
                       reads=[("pt", spar), ("ones",)], writes=["pb5" if par == 0 else "pb6"], tag="p2_sum")

        def emit_epi_a(hh, qb):
            par = hh % 2
            sumb = pb5 if par == 0 else pb6
            st = st_rs if par == 0 else st_aq
            stk = "st_am" if par == 0 else "st_aq"
            bfr = bf_rs if par == 0 else bf_rstd
            bfk = "bf_am" if par == 0 else "bf_rstd"
            sb.add("scalar", lambda e, st=st, sumb=sumb: e.activation(st[:], sumb[0:1, :], AF.Ln,
                                                                      bias=zeros_f[0:1, :], scale=1.0),
                   reads=["pb5" if par == 0 else "pb6", ("zeros",)], writes=[stk], tag="p2_rc")
            sb.add("scalar", lambda e, st=st: e.activation(st[:], st[:], AF.Exp,
                                                           bias=zeros_f[0:1, :], scale=-1.0),
                   reads=[stk, ("zeros",)], writes=[stk], tag="p2_rc2")
            sb.add("vector", lambda e, st=st, bfr=bfr: e.tensor_copy(bfr[:], st[:]),
                   reads=[stk], writes=[bfk], tag="p2_rcb")

        def emit_epi_b(hh, qb):
            par = hh % 2
            sumb = pb5 if par == 0 else pb6
            sumk = "pb5" if par == 0 else "pb6"
            bfr = bf_rs if par == 0 else bf_rstd
            bfk = "bf_am" if par == 0 else "bf_rstd"
            pvb, pvkey = (pb4, "pb4") if par == 0 else (pb7, "pb7")
            bcs = F2[:, 2560:3072] if par == 0 else F2[:, 0:512]
            bck = "rb2" if par == 0 else "mb"
            sb.add("tensor", lambda e, sumb=sumb, bfr=bfr: e.matmul(sumb[:], onesk1[:], bfr[:], start=True, stop=True),
                   reads=[bfk, ("onesk1",)], writes=[sumk], tag="p2_rbc")
            sb.add("vector", lambda e, bcs=bcs, sumb=sumb: e.tensor_copy(bcs, sumb[:]),
                   reads=[sumk], writes=[bck], tag="p2_rbe")
            oo = A4[:, 2048 + par * 512: 2048 + (par + 1) * 512]
            sb.add("vector", lambda e, o=oo, pvb=pvb, bcs=bcs: e.tensor_tensor(o, pvb[:], bcs, ALU.mult),
                   reads=[pvkey, bck], writes=[("oo", par), ("o0", 4 + par)], tag="p2_on")
            sb.dma("gpsimd", lambda e, o=oo, hh=hh, qb=qb: e.dma_start(
                out=o_scr[hh * 128:(hh + 1) * 128, qb * 512:(qb + 1) * 512], in_=o),
                reads=[("oo", par)], writes=[("o_scr", hh, qb)], tag="p2_ost")

        def p2_all(nh):
            entries = []
            for hp in range(nh // 2):
                h0 = 2 * hp
                base = [(qb, kp) for qb in range(NT) for kp in range(2 * (qb + 1))]
                pos = [0, 0]
                cool = [0, 0]
                rt = []
                cur = 0
                while pos[0] < len(base) or pos[1] < len(base):
                    cand = [s for s in (cur, 1 - cur) if pos[s] < len(base)]
                    pick = None
                    for s in cand:
                        if cool[s] == 0:
                            pick = s
                            break
                    if pick is None:
                        pick = cand[0]
                    qb, kp = base[pos[pick]]
                    rt.append(("t", (h0 + pick, qb, kp)))
                    pos[pick] += 1
                    for s in (0, 1):
                        if cool[s]:
                            cool[s] -= 1
                    if kp == 2 * (qb + 1) - 1:
                        cool[pick] = 3
                    cur = 1 - pick
                if hp == 0:
                    entries.append(("q", 0))
                    entries.append(("q", 1))
                    for u in range(8):
                        for hn in (0, 1):
                            entries.append(("k", hn, u))
                            entries.append(("v", hn, u * 4))
                else:
                    entries.append(("q", h0 + 1))
                    for u in range(8):
                        entries.append(("k", h0 + 1, u))
                        entries.append(("v", h0 + 1, u * 4))
                units = []
                if h0 + 2 < nh:
                    units.append(("q", h0 + 2))
                    for u in range(8):
                        units.append(("k", h0 + 2, u))
                        units.append(("v", h0 + 2, u * 4))
                ui = 0
                for i, tsk in enumerate(rt):
                    entries.append(tsk)
                    if i >= 8 and i % 2 == 0 and ui < len(units):
                        entries.append(units[ui])
                        ui += 1
                entries.extend(units[ui:])

            slots = []
            epi_pend = []

            def consume_slot(j):
                kind = slots[j][0]
                if kind != "t":
                    return
                task = slots[j][1]
                p2_emit_consume(task, j % 2)
                hh, pqb, pkp = task
                if pkp == 2 * (pqb + 1) - 1:
                    emit_epi_a(hh, pqb)
                    epi_pend.append((hh, pqb, len(slots) + 1))

            for ent in entries:
                if ent[0] == "q":
                    p2_emit_q(ent[1])
                    continue
                s = len(slots)
                if s >= 2:
                    consume_slot(s - 2)
                slots.append(ent)
                if ent[0] == "t":
                    p2_emit_S(ent[1], s % 2)
                    p2_emit_exp(ent[1], s % 2)
                elif ent[0] == "k":
                    p2_emit_K(ent[1], ent[2], s % 2)
                else:
                    p2_emit_V(ent[1], ent[2], s % 2)
                while epi_pend and epi_pend[0][2] <= s:
                    e0 = epi_pend.pop(0)
                    emit_epi_b(e0[0], e0[1])
            for j in (len(slots) - 2, len(slots) - 1):
                consume_slot(j)
            while epi_pend:
                e0 = epi_pend.pop(0)
                emit_epi_b(e0[0], e0[1])

        def p3_oload(t):
            for a0 in range(0, H, 4):
                sb.dma(("gpsimd", "sync")[(a0 // 4) % 2], lambda e, a0=a0, t=t: e.dma_start(
                    out=A5[:, a0 * 512:(a0 + 4) * 512].rearrange('p (a w) -> p a w', a=4),
                    in_=o_scr[a0 * 128:(a0 + 4) * 128, t * T:(t + 1) * T].rearrange('(a p) w -> p a w', p=128)),
                    reads=[("o_scr", a0 + k, t) for k in range(4)],
                    writes=[("ot", a0 + k) for k in range(4)], tag="p3_old")

        def p3_tile(t):
            otv = lambda a: A5[:, a * 512:(a + 1) * 512]
            for c0 in range(0, NDC, 4):
                sb.dma(("sync", "gpsimd")[(c0 // 4) % 2], lambda e, c0=c0, t=t: e.dma_start(
                    out=A3[:, c0 * 512:(c0 + 4) * 512].rearrange('p (a w) -> p a w', a=4),
                    in_=r_scr[c0 * 128:(c0 + 4) * 128, t * T:(t + 1) * T].rearrange('(a p) w -> p a w', p=128)),
                    reads=[("r_scr", c0 + k, t) for k in range(4)],
                    writes=[rkey(c0 + k) for k in range(4)], tag="p3_rld")
            for dd in range(NDC):
                par = dd % 2
                wbuf = wsb[:, par * 4096: par * 4096 + 2048]
                need(("wo_b", dd, 0), ("wo_b", dd, 1))
                sb.dma("scalar", lambda e, d=wbuf, dd=dd: e.dma_start(out=d, in_=wo_b[dd * 128:(dd + 1) * 128, :]),
                       reads=[("wo_b", dd, 0), ("wo_b", dd, 1)],
                       writes=[("wsb", par)], tag="p3_wol")
                for a in range(H):
                    sb.add("tensor", lambda e, a=a, par=par:
                           e.matmul(PPH[par], wsb[:, par * 4096 + a * 128: par * 4096 + (a + 1) * 128],
                                    otv(a), start=(a == 0), stop=(a == H - 1)),
                           reads=[("wsb", par), ("ot", a)], writes=[KH[par]], tag="p3_wo")
                sb.add("vector", lambda e, dd=dd, par=par: e.tensor_tensor(rview(dd), rview(dd), PPH[par], ALU.add),
                       reads=[KH[par], rkey(dd)], writes=[rkey(dd)], tag="p3_wadd")
                if dd > 1:
                    ln_stats_chunk(dd - 2, rkey, rview, f"p3ln_{t}", with_gate=True, sq_eng="gpsimd")
            if t + 1 < NT:
                p3_oload(t + 1)
            ln_stats_chunk(NDC - 2, rkey, rview, f"p3ln_{t}", with_gate=True, sq_eng="gpsimd")
            ln_stats_chunk(NDC - 1, rkey, rview, f"p3ln_{t}", with_gate=True, sq_eng="gpsimd")
            ln_finish(rkey, rview, xnview, xnkey, f"p3ln_{t}")
            gate_combine(f"p3g_{t}")

            pending = []

            def emit_tr(i, par):
                fin = FA[:, par * 512:(par + 1) * 512]
                for g in range(NG):
                    pbt, pkey = (pb6, "pb6") if g % 2 == 0 else (pb7, "pb7")
                    sb.add("tensor", lambda e, f=fin, g=g, p=pbt: e.transpose(p[:, 0:128], f[:, g * 128:(g + 1) * 128], id128[:]),
                           reads=[("fin", par), ("id",)], writes=[pkey], tag="p3_tr")
                    sb.add("vector", lambda e, g=g, i=i, p=pbt:
                           e.tensor_copy(FA[:, 2560 + g * 512 + (i % 4) * 128: 2560 + g * 512 + (i % 4 + 1) * 128],
                                         p[:, 0:128]),
                           reads=[pkey], writes=[("ystg", g, i % 4)], tag="p3_ye")
                if i % 4 == 3:
                    dd4 = i // 4
                    for g in range(NG):
                        sb.dma("sync", lambda e, g=g, dd4=dd4, t=t: e.dma_start(
                            out=y_out[t * T + g * 128: t * T + (g + 1) * 128, dd4 * 512:(dd4 + 1) * 512],
                            in_=FA[:, 2560 + g * 512: 2560 + (g + 1) * 512]),
                            reads=[("ystg", g, 0), ("ystg", g, 1), ("ystg", g, 2), ("ystg", g, 3)],
                            writes=[("ydma", g)], tag="p3_yst")

            def p3_out(i, par, bb):
                while pending:
                    emit_tr(*pending.pop(0))
                t1 = FA[:, 1024 + par * 512: 1024 + (par + 1) * 512]
                ds = FA[:, 2048:2560]
                fin = FA[:, par * 512:(par + 1) * 512]
                sb.add("vector", lambda e, o=t1, i=i, par=par:
                       e.tensor_scalar(o, PPO[par], bb[:, i:i + 1], 0.5, ALU.add, ALU.mult),
                       reads=[KO[par], ("b1b",)], writes=[("t1", par)], tag="p3_oe1")
                sb.add("vector", lambda e, o=t1, i=i: e.tensor_tensor(ds, A4[:, i * 512:(i + 1) * 512], o, ALU.subtract),
                       reads=[("t1", par), ("o0", i)], writes=[("ds",)], tag="p3_df")
                sb.add("vector", lambda e: e.tensor_tensor(ds, ds, F2[:, 1024:1536], ALU.mult),
                       reads=[("ds",), "maskb"], writes=[("ds",)], tag="p3_sel")
                sb.add("vector", lambda e, o=t1: e.tensor_tensor(ds, ds, o, ALU.add),
                       reads=[("ds",), ("t1", par)], writes=[("ds",)], tag="p3_cmb")
                sb.add("vector", lambda e, f=fin, i=i: e.tensor_tensor(f, rview(i), ds, ALU.add),
                       reads=[("ds",), rkey(i)], writes=[("fin", par)], tag="p3_fin")
                pending.append((i, par))
                if i == NDC - 1:
                    while pending:
                        emit_tr(*pending.pop(0))

            ffn(xnview, xnkey, f"p3f_{t}", p3_out)

        if 0 in phases:
            prelude()
        if 1 in phases:
            p1_xload(0, 0)
            p1_xload(0, 1)
            for t in range(nt1):
                p1_tile(t)
        if 2 in phases:
            pump(len(convq))
            convert_direct(wkvu_s, A3, ("wkvu",), NRC * 4096,
                           extra_writes=[rkey(c) for c in range(NDC)])
            p2_all(nheads)
        if 3 in phases:
            p3_oload(0)
            for t in range(nt3):
                p3_tile(t)

        sb.emit(es.enter_context(nc.Block()), es)
    return nc


def _swizzle(inputs):
    f32 = np.float32
    Wq = np.asarray(inputs["L0_attn_Wq"], f32)
    Wkvd = np.asarray(inputs["L0_attn_Wkv_down"], f32)
    Wkvu = np.asarray(inputs["L0_attn_Wkv_up"], f32)
    Wo = np.asarray(inputs["L0_attn_Wo"], f32)
    Wg = np.asarray(inputs["L0_ffn_Wg"], f32)
    W0a = np.asarray(inputs["L0_ffn_W0a"], f32)
    W0b = np.asarray(inputs["L0_ffn_W0b"], f32)
    W1a = np.asarray(inputs["L0_ffn_W1a"], f32)
    W1b = np.asarray(inputs["L0_ffn_W1b"], f32)
    C = np.ascontiguousarray
    d = {}
    d["wq_s"] = C(Wq.reshape(NDC, 128, H, 128).transpose(2, 1, 0, 3).reshape(H * 128, NDC * 128))
    d["wkvd_s"] = C(Wkvd.reshape(NDC, 128, NRC, 128).transpose(1, 0, 2, 3).reshape(128, NDC * NRC * 128))
    d["wkvu_s"] = C(Wkvu.reshape(NRC, 128, 4096).transpose(1, 0, 2).reshape(128, NRC * 4096))
    d["wo_s"] = C(Wo.reshape(NDC, 128, NDC, 128).transpose(2, 1, 0, 3).reshape(NDC * 128, NDC * 128))
    d["wg_s"] = C(Wg.T.reshape(NDC, 128, 2).transpose(1, 0, 2).reshape(128, NDC * 2))
    d["w0a_s"] = C(W0a.reshape(NFC, 128, NDC, 128).transpose(0, 3, 2, 1).reshape(NFC * 128, NDC * 128))
    d["w1a_s"] = C(W1a.reshape(NFC, 128, NDC, 128).transpose(0, 3, 2, 1).reshape(NFC * 128, NDC * 128))
    d["w0b_s"] = C(W0b.reshape(NDC, 128, NFC, 128).transpose(0, 3, 2, 1).reshape(NDC * 128, NFC * 128))
    d["w1b_s"] = C(W1b.reshape(NDC, 128, NFC, 128).transpose(0, 3, 2, 1).reshape(NDC * 128, NFC * 128))
    d["b0a_s"] = C(np.asarray(inputs["L0_ffn_b0a"], f32).reshape(NFC, 128).T)
    d["b1a_s"] = C(np.asarray(inputs["L0_ffn_b1a"], f32).reshape(NFC, 128).T)
    d["b0b_s"] = C(np.asarray(inputs["L0_ffn_b0b"], f32).reshape(NDC, 128).T)
    d["b1b_s"] = C(np.asarray(inputs["L0_ffn_b1b"], f32).reshape(NDC, 128).T)
    d["kvw_s"] = C(np.asarray(inputs["L0_attn_kv_norm_w"], f32).reshape(NRC, 128).T)
    d["id_s"] = np.eye(128, dtype=f32)
    k = np.arange(128)[:, None]
    q = np.arange(512)[None, :]
    m = np.concatenate([(q >= jj * 128 + k) for jj in range(4)], axis=1)
    d["mask_s"] = m.astype(ml_dtypes.bfloat16)
    return d


def kernel(**inputs):
    x = np.asarray(inputs["L0_attn_x"], np.float32)
    wz = _swizzle(inputs)
    nc = build_nc(debug=False)
    in_maps = []
    for b in range(8):
        m = dict(wz)
        m["x"] = np.ascontiguousarray(x[b])
        in_maps.append(m)
    res = run_bass_kernel_spmd(nc, in_maps, core_ids=list(range(8)))
    out = np.stack([r["y"] for r in res.results], axis=0)
    return out
